# version 02 (final)
# baseline (speedup 1.0000x reference)
import contextlib
import numpy as np

import concourse.bass as bass
import concourse.mybir as mybir
from concourse.bass import IndirectOffsetOnAxis
from concourse.bass_utils import run_bass_kernel_spmd

F32 = mybir.dt.float32
BF16 = mybir.dt.bfloat16
I32 = mybir.dt.int32
ALU = mybir.AluOpType
AF = mybir.ActivationFunctionType
AX = mybir.AxisListType

NC = 8
B = 4
D = 1024
H = 8
DH = 128
NL0 = 4
FF0 = 2048
EMO = 8
FMO = 4096
INNER = 2048
NST = 64
HD2 = 256
KERN = 4
FFD = 4096
CHK = 128
EPS = 1e-5
RC = float(np.float32(1.5 * 2 ** 23))

DBG_SCRATCH = set()
STOP_AFTER = None

ENGINES = ("te", "ve", "se", "ge", "sy")
SEM_NAMES = ["te", "ve", "se",
             "a0", "a1", "b0", "b1", "c0", "c1",
             "wa", "wb", "cs",
             "go", "go1", "go2", "go3",
             "gi0", "gi1", "gc", "gx0", "gx1"]
ORDERED = {"te", "ve", "se"}


class Ev:
    __slots__ = ("sem", "val")

    def __init__(self, sem, val):
        self.sem = sem
        self.val = val


class Buf:
    __slots__ = ("ap", "writers", "readers", "name")

    def __init__(self, ap, name=""):
        self.ap = ap
        self.writers = []
        self.readers = []
        self.name = name


class Prog:
    def __init__(self, nc):
        self.nc = nc
        self.ops = {e: [] for e in ENGINES}
        self.cnt = {s: 0 for s in SEM_NAMES}
        self.last_wait = {}
        self.sems = {}

    def _waits(self, eng, evs):
        need = {}
        for ev in evs:
            if ev is None:
                continue
            sem, val = ev.sem, ev.val
            k = (eng, sem)
            if self.last_wait.get(k, -1) >= val:
                continue
            need[sem] = max(need.get(sem, -1), val)
        for sem, val in need.items():
            self.last_wait[(eng, sem)] = val
            self.ops[eng].append(
                lambda e, s=sem, v=val: e.wait_ge(self.sems[s], v))

    def _update_bufs(self, reads, writes, ev):
        for b in reads:
            b.readers.append(ev)
        for b in writes:
            if b.readers:
                b.writers = [ev]
                b.readers = []
            else:
                b.writers.append(ev)

    def op(self, eng, fn, reads=(), writes=(), sem=None, inc=1):
        assert eng in ("te", "ve", "se", "ge")
        evs = []
        for b in reads:
            evs += b.writers
        for b in writes:
            evs += b.writers
            evs += b.readers
        self._waits(eng, evs)
        sname = sem if sem else eng
        self.cnt[sname] += inc
        v = self.cnt[sname]
        self.ops[eng].append(
            lambda e, f=fn, s=sname, vv=inc: f(e).then_inc(self.sems[s], vv))
        ev = Ev(sname, v)
        self._update_bufs(reads, writes, ev)
        return ev

    def _serialize_stream(self, queue, stream):
        prev = self.cnt[stream]
        if prev > 0 and self.last_wait.get((queue, stream), -1) < prev:
            self.last_wait[(queue, stream)] = prev
            self.ops[queue].append(
                lambda e, s=stream, v=prev: e.wait_ge(self.sems[s], v))

    def dma(self, queue, stream, out, in_, reads=(), writes=()):
        if stream == "go":
            stream = ("go", "go1", "go2", "go3")[self.cnt.setdefault("_gorr", 0) % 4]
            self.cnt["_gorr"] += 1
        oap = out.ap if isinstance(out, Buf) else out
        iap = in_.ap if isinstance(in_, Buf) else in_
        reads = list(reads) + ([in_] if isinstance(in_, Buf) else [])
        writes = list(writes) + ([out] if isinstance(out, Buf) else [])
        evs = []
        for b in reads:
            evs += b.writers
        for b in writes:
            evs += b.writers + b.readers
        self._waits(queue, evs)
        self._serialize_stream(queue, stream)
        self.cnt[stream] += 16
        v = self.cnt[stream]
        self.ops[queue].append(
            lambda e, o=oap, i=iap, s=stream: e.dma_start(out=o, in_=i)
            .then_inc(self.sems[s], 16))
        ev = Ev(stream, v)
        self._update_bufs(reads, writes, ev)
        return ev

    def idma(self, stream, out, idx_buf, in_ap, writes=(), reads=()):
        oap = out.ap if isinstance(out, Buf) else out
        writes = list(writes) + ([out] if isinstance(out, Buf) else [])
        evs = []
        for b in list(reads) + [idx_buf]:
            evs += b.writers
        for b in writes:
            evs += b.writers + b.readers
        self._waits("ge", evs)
        self._serialize_stream("ge", stream)
        self.cnt[stream] += 16
        v = self.cnt[stream]
        self.ops["ge"].append(
            lambda e, o=oap, ix=idx_buf.ap, i=in_ap, s=stream:
            e.indirect_dma_start(o, None, i, IndirectOffsetOnAxis(ix, 0))
            .then_inc(self.sems[s], 16))
        ev = Ev(stream, v)
        idx_buf.readers.append(ev)
        for b in reads:
            b.readers.append(ev)
        self._update_bufs([], writes, ev)
        return ev

    def coll(self, kind, op, ins, outs, groups):
        self._waits("ge", [Ev(s, self.cnt[s])
                           for s in ("go", "go1", "go2", "go3")])
        self._serialize_stream("ge", "gc")
        self.cnt["gc"] += 1
        v = self.cnt["gc"]
        self.ops["ge"].append(
            lambda e, k=kind, o=op, i=ins, u=outs, g=groups:
            e.collective_compute(k, o, ins=i, outs=u, replica_groups=g)
            .then_inc(self.sems["gc"], 1))
        return Ev("gc", v)

    def raw_wait(self, eng, ev):
        self._waits(eng, [ev])

    def barrier(self):
        for eng in ENGINES:
            for s in SEM_NAMES:
                if self.cnt[s] > 0 and self.last_wait.get((eng, s), -1) < self.cnt[s]:
                    self.last_wait[(eng, s)] = self.cnt[s]
                    self.ops[eng].append(
                        lambda e, ss=s, v=self.cnt[s]: e.wait_ge(self.sems[ss], v))


class Pool:

    def __init__(self, nc, tag):
        self.nc = nc
        self.tag = tag
        self.handles = []
        self.n = 0

    def t(self, name, shape, dt=F32):
        g = self.nc.sbuf_tensor(f"{self.tag}_{name}_{self.n}", shape, dt)
        self.n += 1
        h = g.__enter__()
        self.handles.append(g)
        return Buf(h.ap(), name)

    def free(self):
        for g in reversed(self.handles):
            g.__exit__(None, None, None)
        self.handles = []



def gammas_np():
    return (1.0 - np.exp(np.linspace(np.log(1 / 32.), np.log(1 / 512.), H))).astype(np.float64)


def host_constants(S):
    Sc = S // NC
    g = gammas_np()
    c = {}
    c["ident"] = np.eye(128, dtype=np.float32)
    c["identbf"] = np.eye(128).astype(np.float32)
    j = np.arange(CHK)
    c["tri"] = (j[None, :] >= j[:, None]).astype(np.float32)
    c["iota"] = np.broadcast_to(j[None, :].astype(np.float32), (128, 128)).copy()
    Dq = np.zeros((H * CHK, CHK), np.float32)
    Dk = np.zeros((H * CHK, CHK), np.float32)
    for h in range(H):
        Dq[h * CHK + j, j] = g[h] ** j
        Dk[h * CHK + j, j] = g[h] ** (-j.astype(np.float64))
    c["Dq"], c["Dk"] = Dq, Dk
    c["gamcol"] = np.stack([g[h] ** (-j.astype(np.float64)) for h in range(H)],
                           axis=1).astype(np.float32)
    c["gam_chk"] = (g ** CHK).astype(np.float32)
    pos = np.arange(S, dtype=np.float64)
    inv_freq = 1.0 / 10000.0 ** (np.arange(0, DH, 2, dtype=np.float64) / DH)
    ang = pos[:, None] * inv_freq[None, :]
    c["cos"] = np.cos(ang).astype(np.float32)
    c["sin"] = np.sin(ang).astype(np.float32)
    PP = np.zeros((4, H, 128), np.float32)
    for k in range(4):
        PP[k, 2 * k, :64] = 1.0
        PP[k, 2 * k + 1, 64:] = 1.0
    c["PP"] = PP.reshape(4 * H, 128)
    QQ = np.zeros((16, B * H, 128), np.float32)
    for k in range(16):
        QQ[k, 2 * k, :64] = 1.0
        QQ[k, 2 * k + 1, 64:] = 1.0
    c["QQ"] = QQ.reshape(16 * B * H, 128)
    c["ones_1_128"] = np.ones((1, 128), np.float32)
    c["ones8"] = np.ones((H, CHK), np.float32)
    return c


def per_core_constants(cst, S, core):
    Sc = S // NC
    g = gammas_np()
    pc = {}
    pc["cosc"] = np.ascontiguousarray(cst["cos"][core * Sc:(core + 1) * Sc])
    pc["sinc"] = np.ascontiguousarray(cst["sin"][core * Sc:(core + 1) * Sc])
    rw = np.zeros((NC, H), np.float64)
    for cp in range(core):
        rw[cp] = g ** (Sc * (core - 1 - cp))
    pc["retw"] = np.broadcast_to(rw.reshape(1, NC * H).astype(np.float32),
                                 (128, NC * H)).copy()
    pc["m1col"] = np.full((128, 1), 1.0 if core >= 1 else 0.0, np.float32)
    pc["m2col"] = np.full((128, 1), 1.0 if core >= 2 else 0.0, np.float32)
    prev = max(core - 1, 0)
    prev2 = max(core - 2, 0)
    pc["ssd_i1"] = (prev * (B * H * 64 // 128) * 128 // 128 * 0 +
                    np.arange(B * H * 64 // 2, dtype=np.int32) * 0)
    pc["ssd_i1"] = (prev * B * H * NST + np.arange(B * H * NST, dtype=np.int32)).reshape(-1, 1)
    pc["ssd_i2"] = (prev2 * B * H * NST + np.arange(B * H * NST, dtype=np.int32)).reshape(-1, 1)
    pc["ll_i1"] = (prev * B * H + np.arange(B * H, dtype=np.int32)).reshape(-1, 1)
    pc["halo_i"] = (prev * B * 16 + np.arange(B * 16, dtype=np.int32)).reshape(-1, 1)
    return pc


def moe_host_prep(top_idx, S):
    Sc = S // NC
    JB = Sc // CHK
    tok = top_idx.astype(np.int64).T
    Cmo = tok.shape[1]
    N0 = int(tok.max()) + 1
    assert N0 <= B * S
    bb = tok // S
    ss = tok % S
    owner = ss // Sc
    loc = (bb * Sc + ss % Sc).astype(np.int64)
    cnt = np.zeros((NC, EMO), np.int64)
    for c in range(NC):
        for e in range(EMO):
            cnt[c, e] = int((owner[e] == c).sum())
    cap = int(max(128, ((int(cnt.max()) + 127) // 128) * 128))
    gidx = np.zeros((NC, EMO, cap), np.int32)
    widx = np.zeros((NC, EMO, cap), np.int32)
    srcpos = {}
    for c in range(NC):
        for e in range(EMO):
            m = np.nonzero(owner[e] == c)[0]
            gidx[c, e, :len(m)] = loc[e, m]
            widx[c, e, :len(m)] = tok[e, m]
            for i, ci in enumerate(m):
                srcpos[(c, e, int(ci))] = e * cap + i
    ntile = B * Sc // CHK
    lists = [[[] for _ in range(ntile)] for _ in range(NC)]
    for e in range(EMO):
        for ci in range(Cmo):
            c = int(owner[e, ci])
            lr = int(loc[e, ci])
            lists[c][lr // CHK].append((srcpos[(c, e, ci)], lr % CHK))
    chunks_t = [max(1 if max(len(lists[c][t]) for c in range(NC)) > 0 else 0,
                    (max(len(lists[c][t]) for c in range(NC)) + 127) // 128)
                for t in range(ntile)]
    work_tiles = [t for t in range(ntile) if chunks_t[t] > 0]
    sidx = np.zeros((NC, len(work_tiles), max(chunks_t) if work_tiles else 1, 128), np.int32)
    soff = np.full((NC, len(work_tiles), max(chunks_t) if work_tiles else 1, 128), -1.0, np.float32)
    for c in range(NC):
        for wi, t in enumerate(work_tiles):
            for i, (sp, off) in enumerate(lists[c][t]):
                sidx[c, wi, i // 128, i % 128] = sp
                soff[c, wi, i // 128, i % 128] = float(off)
    return dict(cap=cap, work_tiles=work_tiles, chunks_t=chunks_t,
                sidx=sidx, soff=soff, gidx=gidx, widx=widx, ntile=ntile)



class K:

    def __init__(self, S, moe):
        self.S = S
        self.Sc = S // NC
        self.JB = self.Sc // CHK
        self.NCH = B * self.JB
        self.moe = moe
        nc = bass.Bass("TRN2", target_bir_lowering=False, debug=False)
        self.nc = nc
        self.P = Prog(nc)
        self.g = gammas_np()
        self._declare_io()
        self._psum()
        self.stack = contextlib.ExitStack()
        for s in SEM_NAMES:
            self.P.sems[s] = self.stack.enter_context(nc.semaphore(s))

    def _declare_io(self):
        nc = self.nc
        Sc, JB, NCH = self.Sc, self.JB, self.NCH
        R = B * Sc

        def inp(name, shape, dt=F32):
            return nc.dram_tensor(name, shape, dt, kind="ExternalInput").ap()
        self.i = {}
        i = self.i
        i["x0"] = inp("x0", [R, D])
        i["wq"] = inp("wq", [NL0, D, D]); i["wk"] = inp("wk", [NL0, D, D])
        i["wv"] = inp("wv", [NL0, D, D]); i["wg"] = inp("wg", [NL0, D, D])
        i["wo"] = inp("wo", [NL0, D, D])
        i["w1a"] = inp("w1a", [NL0, D, FF0]); i["w2a"] = inp("w2a", [NL0, FF0, D])
        i["lnsb"] = inp("lnsb", [128, D]); i["lnbb"] = inp("lnbb", [128, D])
        i["gnsb"] = inp("gnsb", [NL0, 128, D]); i["gnbb"] = inp("gnbb", [NL0, 128, D])
        i["mw1"] = inp("mw1", [EMO, D, FMO]); i["mw2"] = inp("mw2", [EMO, FMO, D])
        i["mb1"] = inp("mb1", [EMO, 128, FMO]); i["mb2"] = inp("mb2", [EMO, 128, D])
        i["cw"] = inp("cw", [self.moe["cwN"], EMO])
        i["wxT"] = inp("wxT", [D, INNER]); i["wzT"] = inp("wzT", [D, INNER])
        i["wbT"] = inp("wbT", [INNER, H * NST]); i["wcT"] = inp("wcT", [INNER, H * NST])
        i["wdtT"] = inp("wdtT", [INNER, H])
        i["woutT"] = inp("woutT", [INNER, D])
        i["convw"] = inp("convw", [INNER, KERN]); i["convb"] = inp("convb", [128, 16])
        i["bdt"] = inp("bdt", [H, 1]); i["loga"] = inp("loga", [H, 1])
        i["dpar"] = inp("dpar", [128, H])
        i["gnw2"] = inp("gnw2", [128, INNER]); i["gnb2"] = inp("gnb2", [128, INNER])
        i["dw1T"] = inp("dw1T", [D, FFD]); i["dw2T"] = inp("dw2T", [FFD, D])
        i["db1"] = inp("db1", [128, FFD]); i["db2"] = inp("db2", [128, D])
        i["ident"] = inp("ident", [128, 128]); i["identbf"] = inp("identbf", [128, 128], BF16)
        i["tri"] = inp("tri", [128, 128]); i["iota"] = inp("iota", [128, 128])
        i["Dq"] = inp("Dq", [H * 128, 128]); i["Dk"] = inp("Dk", [H * 128, 128])
        i["gamcol"] = inp("gamcol", [128, H])
        i["cosc"] = inp("cosc", [Sc, 64]); i["sinc"] = inp("sinc", [Sc, 64])
        i["retw"] = inp("retw", [128, NC * H])
        i["m1col"] = inp("m1col", [128, 1]); i["m2col"] = inp("m2col", [128, 1])
        i["PP"] = inp("PP", [4 * H, 128]); i["QQ"] = inp("QQ", [16 * B * H, 128])
        i["ones_1_128"] = inp("ones_1_128", [1, 128])
        i["ones8"] = inp("ones8", [H, CHK])
        i["ssd_i1"] = inp("ssd_i1", [B * H * NST, 1], I32)
        i["ssd_i2"] = inp("ssd_i2", [B * H * NST, 1], I32)
        i["ll_i1"] = inp("ll_i1", [B * H, 1], I32)
        i["halo_i"] = inp("halo_i", [B * 16, 1], I32)
        m = self.moe
        i["gidx"] = inp("gidx", [EMO * m["cap"], 1], I32)
        i["widx"] = inp("widx", [EMO * m["cap"], 1], I32)
        nw = max(1, len(m["work_tiles"]) * max(m["chunks_t"] + [1]))
        i["sidx"] = inp("sidx", [nw * 128, 1], I32)
        i["soff"] = inp("soff", [nw * 128, 1])
        self.y = nc.dram_tensor("y", [R, D], F32, kind="ExternalOutput").ap()
        def scr(name, shape, dt=F32, **kw):
            kind = "ExternalOutput" if name in DBG_SCRATCH and not kw else "Internal"
            return nc.dram_tensor(name, shape, dt, kind=kind, **kw).ap()
        self.s = {}
        s = self.s
        s["XA"] = scr("XA", [R, D]); s["SY"] = scr("SY", [R, D])
        s["SQ"] = scr("SQ", [R, D]); s["SYR"] = scr("SYR", [R, D])
        s["X1"] = scr("X1", [R, D]); s["X2"] = scr("X2", [R, D])
        s["BW"] = scr("BW", [EMO * m["cap"], D])
        s["XPT"] = scr("XPT", [B * INNER, Sc])
        s["ZP"] = scr("ZP", [R, INNER]); s["YC"] = scr("YC", [R, INNER])
        s["CFT"] = scr("CFT", [NCH * 512, CHK])
        s["X3"] = scr("X3", [R, D])
        for li in range(NL0):
            s[f"rsi{li}"] = scr(f"rsi{li}", [B * H * DH, DH])
            s[f"rso{li}"] = scr(f"rso{li}", [NC * B * H * DH, DH], addr_space="Local")
        s["hin"] = scr("hin", [B * 16, 384])
        s["hout"] = scr("hout", [NC * B * 16, 384], addr_space="Local")
        s["ssin"] = scr("ssin", [B * H * NST, HD2])
        s["ssout"] = scr("ssout", [NC * B * H * NST, HD2], addr_space="Local")
        s["llin"] = scr("llin", [B * H, 1])
        s["llout"] = scr("llout", [NC * B * H, 1], addr_space="Local")
        if "DUMMY" in DBG_SCRATCH:
            nc.dram_tensor("DUMMY", [1, 4], F32, kind="ExternalOutput").ap()
        for nm in DBG_SCRATCH:
            if nm.endswith("d0") or nm.endswith("d1") or nm.endswith("d2") or nm.endswith("d3"):
                s[nm] = scr(nm, [R, D])
        s["srin"] = scr("srin", [1, B * 16])
        s["srout"] = scr("srout", [1, B * 16], addr_space="Local")

    def _psum(self):
        self.pb = [Buf(self.nc.alloc_psum_tensor(f"pb{k}", [128, 512], F32).ap(),
                       f"pb{k}") for k in range(8)]

    def vop(self, fn, reads=(), writes=()):
        return self.P.op("ve", fn, reads, writes)

    def sop(self, fn, reads=(), writes=()):
        return self.P.op("se", fn, reads, writes)

    def top(self, fn, reads=(), writes=()):
        return self.P.op("te", fn, reads, writes)

    def ts2(self, dst, dst_ap, src, src_ap, s1buf, s1_ap, s2_ap, s2buf=None):
        s2buf = s2buf or s1buf
        self.vop(lambda e, o=dst_ap, ii=src_ap, s=s1_ap:
                 e.tensor_single_scalar(o, ii, s, ALU.add),
                 reads=[src, s1buf], writes=[dst])
        self.vop(lambda e, o=dst_ap, ii=dst_ap, s=s2_ap:
                 e.tensor_single_scalar(o, ii, s, ALU.mult),
                 reads=[dst, s2buf], writes=[dst])

    def mm(self, bank, oap, lhsT, rhs, start, stop, reads=()):
        return self.top(lambda e, o=oap, l=lhsT, r=rhs, st=start, sp=stop:
                        e.matmul(o, l, r, start=st, stop=sp),
                        reads=reads, writes=[bank])

    def transpose(self, bank, oap, in_ap, ident_ap, reads=()):
        return self.top(lambda e, o=oap, i=in_ap, d=ident_ap:
                        e.transpose(o, i, d), reads=reads, writes=[bank])

    def mm_diag(self, bank, oap, in_ap, diag_ap, reads=()):
        return self.top(lambda e, o=oap, i=in_ap, d=diag_ap:
                        e.matmul(o, i, d, start=True, stop=True),
                        reads=reads, writes=[bank])

    def layernorm(self, dst, src, W, tp, scale=None, bias=None):
        red = tp["red"]; xm = tp["xm"]; sq = tp["sq"]; rst = tp["rst"]
        self.vop(lambda e, o=red.ap[:, 0:1], i=src.ap: e.tensor_reduce(o, i, AX.X, ALU.add),
                 reads=[src], writes=[red])
        self.vop(lambda e, o=red.ap[:, 1:2], i=red.ap[:, 0:1]:
                 e.tensor_scalar(o, i, -1.0 / W, None, ALU.mult),
                 reads=[red], writes=[red])
        self.vop(lambda e, o=xm.ap[:, :W], i=src.ap, s=red.ap[:, 1:2]:
                 e.tensor_single_scalar(o, i, s, ALU.add),
                 reads=[src, red], writes=[xm])
        self.sop(lambda e, o=sq.ap[:, :W], i=xm.ap[:, :W], a=red.ap[:, 2:3]:
                 e.activation(o, i, AF.Square, accum_out=a),
                 reads=[xm], writes=[sq, red])
        self.vop(lambda e, o=red.ap[:, 0:1], i=red.ap[:, 2:3]:
                 e.tensor_scalar(o, i, 1.0 / W, EPS, ALU.mult, ALU.add),
                 reads=[red], writes=[red])
        self.sop(lambda e, o=red.ap[:, 0:1], i=red.ap[:, 0:1]:
                 e.activation(o, i, AF.Sqrt), reads=[red], writes=[red])
        self.vop(lambda e, o=rst.ap[:, 0:1], i=red.ap[:, 0:1]:
                 e.reciprocal(o, i), reads=[red], writes=[rst])
        if scale is None:
            self.vop(lambda e, o=dst.ap[:, :W] if dst.ap.shape[1] >= W else dst.ap,
                     i=xm.ap[:, :W], s=rst.ap[:, 0:1]:
                     e.tensor_single_scalar(o, i, s, ALU.mult),
                     reads=[xm, rst], writes=[dst])
        else:
            self.vop(lambda e, o=xm.ap[:, :W], i=xm.ap[:, :W], s=rst.ap[:, 0:1]:
                     e.tensor_single_scalar(o, i, s, ALU.mult),
                     reads=[xm, rst], writes=[xm])
            self.vop(lambda e, o=xm.ap[:, :W], a=xm.ap[:, :W], b=scale.ap[:, :W]:
                     e.tensor_tensor(o, a, b, ALU.mult),
                     reads=[xm, scale], writes=[xm])
            self.vop(lambda e, o=dst.ap[:, :W], a=xm.ap[:, :W], b=bias.ap[:, :W]:
                     e.tensor_tensor(o, a, b, ALU.add),
                     reads=[xm, bias], writes=[dst])

    def ln_tmp(self, pool, tag=""):
        return {"red": pool.t("red" + tag, [128, 3]),
                "xm": pool.t("xm" + tag, [128, 2048]),
                "sq": pool.t("sq" + tag, [128, 2048]),
                "rst": pool.t("rst" + tag, [128, 1])}

    def transpose_tile(self, dst, src, nblk, ident, banks=(6, 7), reads_extra=()):
        bf = src.ap.dtype == BF16
        for k in range(nblk):
            bank = self.pb[banks[k % len(banks)]]
            ps = (bank.ap.bitcast(BF16) if bf else bank.ap)[:, 0:128]
            self.transpose(bank, ps, src.ap[:, k * 128:(k + 1) * 128], ident.ap,
                           reads=[src, ident] + list(reads_extra))
            self.vop(lambda e, o=dst.ap[:, k * 128:(k + 1) * 128], i=ps:
                     e.tensor_copy(o, i), reads=[bank], writes=[dst])

    def gemm(self, banks_out, lhsT, wsb, nk, Wrow, c0, Wout, extra_reads=()):
        for k in range(nk):
            l = lhsT.ap[:, k * 128:(k + 1) * 128]
            for (bank, pap, cc) in banks_out:
                w = min(512, Wout - cc)
                r = wsb.ap[:, k * Wrow + c0 + cc: k * Wrow + c0 + cc + w]
                self.mm(bank, pap[:, 0:w], l, r, start=(k == 0), stop=(k == nk - 1),
                        reads=[lhsT, wsb] + list(extra_reads))



def dma_kp(P, queue, stream, buf, dram_ap, p=128):
    k = dram_ap.shape[0] // p
    iap = dram_ap.rearrange("(k p) c -> p k c", p=p)
    oap = buf.ap.rearrange("p (k c) -> p k c", k=k) if k > 1 else buf.ap
    iap = iap[:, 0, :] if k == 1 else iap
    P.dma(queue, stream, oap, iap, writes=[buf])


def load_weight(kk, pool, name, dram_ap, rows, cols, tag, stream="cs"):
    nk = rows // 128
    b = pool.t(tag, [128, nk * cols])
    iap = dram_ap.rearrange("(k p) c -> p k c", p=128)
    oap = b.ap.rearrange("p (k c) -> p k c", k=nk)
    kk.P.dma("sy", stream, oap, iap, writes=[b])
    return b


def stage_P0(kk, pools):
    P, s, i = kk.P, kk.s, kk.i
    pool = Pool(kk.nc, "p0")
    tp = kk.ln_tmp(pool)
    xs = [pool.t(f"x{p}", [128, D]) for p in range(2)]
    out = [pool.t(f"o{p}", [128, D]) for p in range(2)]
    for ch in range(kk.NCH):
        p = ch % 2
        P.dma("sy", f"a{p}", xs[p], i["x0"][ch * 128:(ch + 1) * 128, :])
        kk.layernorm(out[p], xs[p], D, tp)
        P.dma("ge", "go", s["XA"][ch * 128:(ch + 1) * 128, :], out[p])
    P.barrier()
    pool.free()


def stage_A_S1(kk, li):
    P, s, i, pb = kk.P, kk.s, kk.i, kk.pb
    pool = Pool(kk.nc, f"s1_{li}")
    wq = load_weight(kk, pool, "wq", i["wq"][li], D, D, "wq")
    wk = load_weight(kk, pool, "wk", i["wk"][li], D, D, "wk")
    wv = load_weight(kk, pool, "wv", i["wv"][li], D, D, "wv")
    lns = pool.t("lns", [128, D]); lnb = pool.t("lnb", [128, D])
    P.dma("sy", "cs", lns, i["lnsb"]); P.dma("sy", "cs", lnb, i["lnbb"])
    ident = pool.t("id", [128, 128]); P.dma("sy", "cs", ident, i["ident"])
    tri = pool.t("tri", [128, 128]); P.dma("sy", "cs", tri, i["tri"])
    dq = pool.t("dq", [128, H * 128]); dk = pool.t("dk", [128, H * 128])
    dma_kp(P, "sy", "cs", dq, i["Dq"])
    dma_kp(P, "sy", "cs", dk, i["Dk"])
    gcol = pool.t("gcol", [128, H]); P.dma("sy", "cs", gcol, i["gamcol"])
    cosb = pool.t("cos", [128, kk.JB * 64]); sinb = pool.t("sin", [128, kk.JB * 64])
    dma_kp(P, "sy", "cs", cosb, i["cosc"])
    dma_kp(P, "sy", "cs", sinb, i["sinc"])
    tp = kk.ln_tmp(pool)
    S_b = [pool.t(f"S{b}", [128, H * 128]) for b in range(B)]
    xa = [pool.t(f"xa{p}", [128, D]) for p in range(2)]
    hbuf = [pool.t(f"h{p}", [128, D]) for p in range(2)]
    hT = [pool.t(f"hT{p}", [128, D]) for p in range(2)]
    qr = pool.t("qr", [128, D]); kr = pool.t("kr", [128, D])
    vsb = pool.t("vsb", [128, D])
    qsT = [pool.t(f"qsT{p}", [128, D]) for p in range(2)]
    atts = pool.t("atts", [128, 128])
    ksT = pool.t("ksT", [128, 128])
    kw = pool.t("kw", [128, 128])
    ysb = [pool.t(f"ysb{p}", [128, D]) for p in range(2)]
    st_t = pool.t("st_t", [128, 128])

    for b in range(B):
        P.op("ve", lambda e, o=S_b[b].ap: e.memset(o, 0.0), writes=[S_b[b]])
        for j in range(kk.JB):
            ch = b * kk.JB + j
            p = ch % 2
            rows = slice(ch * 128, (ch + 1) * 128)
            P.dma("sy", f"a{p}", xa[p], s["XA"][rows, :])
            kk.layernorm(hbuf[p], xa[p], D, tp, scale=lns, bias=lnb)
            kk.transpose_tile(hT[p], hbuf[p], 8, ident, banks=(6, 7))
            bo_q = [(pb[0], pb[0].ap, 0), (pb[1], pb[1].ap, 512)]
            bo_k = [(pb[2], pb[2].ap, 0), (pb[3], pb[3].ap, 512)]
            bo_v = [(pb[4], pb[4].ap, 0), (pb[5], pb[5].ap, 512)]
            for k in range(8):
                l = hT[p].ap[:, k * 128:(k + 1) * 128]
                for wsb, bo in ((wq, bo_q), (wk, bo_k), (wv, bo_v)):
                    for (bank, pap, cc) in bo:
                        kk.mm(bank, pap, l, wsb.ap[:, k * D + cc:k * D + cc + 512],
                              start=(k == 0), stop=(k == 7), reads=[hT[p], wsb])
            for hf in range(2):
                bank = pb[4 + hf]
                kk.vop(lambda e, o=vsb.ap[:, hf * 512:(hf + 1) * 512], i=bank.ap:
                       e.tensor_copy(o, i), reads=[bank], writes=[vsb])
            for (srcb, dst, sgn) in ((pb[0], qr, 1.0), (pb[2], kr, -1.0)):
                for h in range(H):
                    base = h * 128
                    bidx = 0 if base < 512 else 1
                    boff = base % 512
                    srcbank = pb[bidx] if srcb is pb[0] else pb[2 + bidx]
                    x1 = srcbank.ap[:, boff:boff + 128].rearrange("p (n two) -> p n two", two=2)[:, :, 0]
                    x2 = srcbank.ap[:, boff:boff + 128].rearrange("p (n two) -> p n two", two=2)[:, :, 1]
                    cosv = cosb.ap[:, j * 64:(j + 1) * 64]
                    sinv = sinb.ap[:, j * 64:(j + 1) * 64]
                    o = dst.ap[:, base:base + 128].rearrange("p (n two) -> p n two", two=2)
                    t1 = tp["xm"].ap[:, 0:64]; t2 = tp["xm"].ap[:, 64:128]
                    kk.vop(lambda e, oo=t1, a=x1, c=cosv: e.tensor_tensor(oo, a, c, ALU.mult),
                           reads=[srcbank, cosb], writes=[tp["xm"]])
                    kk.vop(lambda e, oo=t2, a=x2, c=sinv, sg=sgn:
                           e.tensor_tensor(oo, a, c, ALU.mult),
                           reads=[srcbank, sinb], writes=[tp["xm"]])
                    if sgn > 0:
                        kk.vop(lambda e, oo=o[:, :, 0], a=t1, c=t2:
                               e.tensor_tensor(oo, a, c, ALU.subtract),
                               reads=[tp["xm"]], writes=[dst])
                    else:
                        kk.vop(lambda e, oo=o[:, :, 0], a=t1, c=t2:
                               e.tensor_tensor(oo, a, c, ALU.add),
                               reads=[tp["xm"]], writes=[dst])
                    kk.vop(lambda e, oo=t1, a=x1, c=sinv: e.tensor_tensor(oo, a, c, ALU.mult),
                           reads=[srcbank, sinb], writes=[tp["xm"]])
                    kk.vop(lambda e, oo=t2, a=x2, c=cosv: e.tensor_tensor(oo, a, c, ALU.mult),
                           reads=[srcbank, cosb], writes=[tp["xm"]])
                    if sgn > 0:
                        kk.vop(lambda e, oo=o[:, :, 1], a=t1, c=t2:
                               e.tensor_tensor(oo, a, c, ALU.add),
                               reads=[tp["xm"]], writes=[dst])
                    else:
                        kk.vop(lambda e, oo=o[:, :, 1], a=t2, c=t1:
                               e.tensor_tensor(oo, a, c, ALU.subtract),
                               reads=[tp["xm"]], writes=[dst])
            for h in range(H):
                hs = slice(h * 128, (h + 1) * 128)
                bq = pb[6]
                kk.mm_diag(bq, bq.ap[:, 0:128], qr.ap[:, hs], dq.ap[:, hs],
                           reads=[qr, dq])
                kk.vop(lambda e, o=qsT[p].ap[:, hs], i=bq.ap[:, 0:128]:
                       e.tensor_copy(o, i), reads=[bq], writes=[qsT[p]])
                bk = pb[7]
                kk.mm_diag(bk, bk.ap[:, 0:128], kr.ap[:, hs], dk.ap[:, hs],
                           reads=[kr, dk])
                kk.vop(lambda e, o=ksT.ap, i=bk.ap[:, 0:128]:
                       e.tensor_copy(o, i), reads=[bk], writes=[ksT])
                ba = pb[0] if h % 2 == 0 else pb[1]
                kk.mm(ba, ba.ap[:, 0:128], ksT.ap, qsT[p].ap[:, hs], True, True,
                      reads=[ksT, qsT[p]])
                kk.vop(lambda e, o=atts.ap, a=ba.ap[:, 0:128], t=tri.ap:
                       e.tensor_tensor(o, a, t, ALU.mult),
                       reads=[ba, tri], writes=[atts])
                by = pb[2] if h % 2 == 0 else pb[3]
                kk.mm(by, by.ap[:, 0:128], atts.ap, vsb.ap[:, hs], True, False,
                      reads=[atts, vsb])
                kk.mm(by, by.ap[:, 0:128], qsT[p].ap[:, hs], S_b[b].ap[:, hs],
                      False, True, reads=[qsT[p], S_b[b]])
                kk.vop(lambda e, o=ysb[p].ap[:, hs], i=by.ap[:, 0:128]:
                       e.tensor_copy(o, i), reads=[by], writes=[ysb[p]])
                kk.vop(lambda e, o=kw.ap, i=kr.ap[:, hs], sc=gcol.ap[:, h:h + 1]:
                       e.tensor_single_scalar(o, i, sc, ALU.mult),
                       reads=[kr, gcol], writes=[kw])
                bs_ = pb[4] if h % 2 == 0 else pb[5]
                kk.mm(bs_, bs_.ap[:, 0:128], kw.ap, vsb.ap[:, hs], True, True,
                      reads=[kw, vsb])
                kk.vop(lambda e, o=st_t.ap, a=S_b[b].ap[:, hs], c=bs_.ap[:, 0:128]:
                       e.tensor_tensor(o, a, c, ALU.add),
                       reads=[S_b[b], bs_], writes=[st_t])
                kk.vop(lambda e, o=S_b[b].ap[:, hs], i=st_t.ap, gg=float(kk.g[h] ** CHK):
                       e.tensor_scalar(o, i, gg, None, ALU.mult),
                       reads=[st_t], writes=[S_b[b]])
            P.dma("ge", "go", s["SY"][rows, :], ysb[p])
            P.dma("ge", "go", s["SQ"][rows, :], qsT[p])
            if f"SYd{li}" in kk.s:
                P.dma("ge", "go", s[f"SYd{li}"][rows, :], ysb[p])
            if f"SQd{li}" in kk.s:
                P.dma("ge", "go", s[f"SQd{li}"][rows, :], qsT[p])
        for h in range(H):
            P.dma("ge", "go", s[f"rsi{li}"][(b * H + h) * DH:(b * H + h + 1) * DH, :],
                  Buf(S_b[b].ap[:, h * 128:(h + 1) * 128], "sv"),
                  reads=[S_b[b]])
    kk.P.coll("AllGather", ALU.bypass, [s[f"rsi{li}"]], [s[f"rso{li}"]],
              [list(range(NC))])
    P.barrier()
    pool.free()


def stage_A_S2(kk, li, pools_sin):
    P, s, i, pb = kk.P, kk.s, kk.i, kk.pb
    pool = Pool(kk.nc, f"s2_{li}")
    wg = load_weight(kk, pool, "wg", i["wg"][li], D, D, "wg")
    wo = load_weight(kk, pool, "wo", i["wo"][li], D, D, "wo")
    lns = pool.t("lns", [128, D]); lnb = pool.t("lnb", [128, D])
    P.dma("sy", "cs", lns, i["lnsb"]); P.dma("sy", "cs", lnb, i["lnbb"])
    gns = pool.t("gns", [128, D]); gnb = pool.t("gnb", [128, D])
    P.dma("sy", "cs", gns, i["gnsb"][li]); P.dma("sy", "cs", gnb, i["gnbb"][li])
    ident = pool.t("id", [128, 128]); P.dma("sy", "cs", ident, i["ident"])
    retw = pool.t("retw", [128, NC * H]); P.dma("sy", "cs", retw, i["retw"])
    tp = kk.ln_tmp(pool)
    sin_b = [pool.t(f"sin{b}", [128, H * 128]) for b in range(B)]
    tmp_s = pool.t("tmps", [128, 128])
    tmp_s2 = pool.t("tmps2", [128, 128])
    for b in range(B):
        P.op("ve", lambda e, o=sin_b[b].ap: e.memset(o, 0.0), writes=[sin_b[b]])
        for cp in range(NC):
            for h in range(H):
                gap = s[f"rso{li}"][(cp * B * H + b * H + h) * DH:
                                    (cp * B * H + b * H + h + 1) * DH, :]
                P.dma("ge", "gx0", tmp_s, gap)
                kk.vop(lambda e, o=tmp_s2.ap, a=tmp_s.ap, sc=retw.ap[:, cp * H + h:cp * H + h + 1]:
                       e.tensor_single_scalar(o, a, sc, ALU.mult),
                       reads=[tmp_s, retw], writes=[tmp_s2])
                kk.vop(lambda e, o=sin_b[b].ap[:, h * 128:(h + 1) * 128],
                       a=sin_b[b].ap[:, h * 128:(h + 1) * 128], c=tmp_s2.ap:
                       e.tensor_tensor(o, a, c, ALU.add),
                       reads=[tmp_s2, sin_b[b]], writes=[sin_b[b]])
    xa = [pool.t(f"xa{p}", [128, D]) for p in range(2)]
    ysb = [pool.t(f"ys{p}", [128, D]) for p in range(2)]
    qsT = [pool.t(f"qs{p}", [128, D]) for p in range(2)]
    hbuf = pool.t("h", [128, D]); hT = pool.t("hT", [128, D])
    gs = pool.t("gs", [128, D])
    yf = pool.t("yf", [128, D])
    mu = pool.t("mu", [128, H]); s2t = pool.t("s2t", [128, H])
    m1 = pool.t("m1", [128, H]); vv = pool.t("vv", [128, H])
    rstd = pool.t("rstd", [128, H]); negmu = pool.t("negmu", [128, H])
    yn = pool.t("yn", [128, D]); gy = pool.t("gy", [128, D])
    gyT = pool.t("gyT", [128, D])
    yr = [pool.t(f"yr{p}", [128, D]) for p in range(2)]
    for b in range(B):
        for j in range(kk.JB):
            ch = b * kk.JB + j
            p = ch % 2
            rows = slice(ch * 128, (ch + 1) * 128)
            P.dma("sy", f"a{p}", xa[p], s["XA"][rows, :])
            P.dma("sy", f"b{p}", ysb[p], s["SY"][rows, :])
            P.dma("sy", f"c{p}", qsT[p], s["SQ"][rows, :])
            kk.layernorm(hbuf, xa[p], D, tp, scale=lns, bias=lnb)
            kk.transpose_tile(hT, hbuf, 8, ident, banks=(6, 7))
            bo = [(pb[0], pb[0].ap, 0), (pb[1], pb[1].ap, 512)]
            kk.gemm(bo, hT, wg, 8, D, 0, D)
            for hf in range(2):
                kk.sop(lambda e, o=gs.ap[:, hf * 512:(hf + 1) * 512], ii=pb[hf].ap:
                       e.activation(o, ii, AF.Silu), reads=[pb[hf]], writes=[gs])
            for h in range(H):
                hs = slice(h * 128, (h + 1) * 128)
                bc = pb[2] if h % 2 == 0 else pb[3]
                kk.mm(bc, bc.ap[:, 0:128], qsT[p].ap[:, hs], sin_b[b].ap[:, hs],
                      True, True, reads=[qsT[p], sin_b[b]])
                kk.vop(lambda e, o=yf.ap[:, hs], a=bc.ap[:, 0:128], c=ysb[p].ap[:, hs]:
                       e.tensor_tensor(o, a, c, ALU.add),
                       reads=[bc, ysb[p]], writes=[yf])
            for h in range(H):
                kk.vop(lambda e, o=sin_b[b].ap[:, h * 128:(h + 1) * 128],
                       i=sin_b[b].ap[:, h * 128:(h + 1) * 128],
                       gg=float(kk.g[h] ** CHK):
                       e.tensor_scalar(o, i, gg, None, ALU.mult),
                       reads=[sin_b[b]], writes=[sin_b[b]])
            yfw = yf.ap.rearrange("p (h d) -> p h d", h=H)
            kk.vop(lambda e, o=mu.ap, ii=yfw: e.tensor_reduce(o, ii, AX.X, ALU.add),
                   reads=[yf], writes=[mu])
            kk.sop(lambda e, o=tp["sq"].ap[:, :D], ii=yf.ap: e.activation(o, ii, AF.Square),
                   reads=[yf], writes=[tp["sq"]])
            sqw = tp["sq"].ap[:, :D].rearrange("p (h d) -> p h d", h=H)
            kk.vop(lambda e, o=s2t.ap, ii=sqw: e.tensor_reduce(o, ii, AX.X, ALU.add),
                   reads=[tp["sq"]], writes=[s2t])
            kk.vop(lambda e, o=negmu.ap, ii=mu.ap: e.tensor_scalar(o, ii, -1.0 / DH, None, ALU.mult),
                   reads=[mu], writes=[negmu])
            kk.vop(lambda e, o=m1.ap, ii=mu.ap: e.tensor_scalar(o, ii, 1.0 / DH, None, ALU.mult),
                   reads=[mu], writes=[m1])
            kk.vop(lambda e, o=m1.ap, a=m1.ap, c=m1.ap: e.tensor_tensor(o, a, c, ALU.mult),
                   reads=[m1], writes=[m1])
            kk.vop(lambda e, o=vv.ap, ii=s2t.ap: e.tensor_scalar(o, ii, 1.0 / DH, None, ALU.mult),
                   reads=[s2t], writes=[vv])
            kk.vop(lambda e, o=vv.ap, a=vv.ap, c=m1.ap: e.tensor_tensor(o, a, c, ALU.subtract),
                   reads=[vv, m1], writes=[vv])
            kk.vop(lambda e, o=rstd.ap, ii=vv.ap: e.tensor_scalar(o, ii, EPS, None, ALU.add),
                   reads=[vv], writes=[rstd])
            kk.sop(lambda e, o=rstd.ap, ii=rstd.ap: e.activation(o, ii, AF.Sqrt),
                   reads=[rstd], writes=[rstd])
            kk.vop(lambda e, o=rstd.ap, ii=rstd.ap: e.reciprocal(o, ii),
                   reads=[rstd], writes=[rstd])
            for h in range(H):
                hs = slice(h * 128, (h + 1) * 128)
                kk.ts2(yn, yn.ap[:, hs], yf, yf.ap[:, hs],
                       negmu, negmu.ap[:, h:h + 1], rstd.ap[:, h:h + 1],
                       s2buf=rstd)
            kk.vop(lambda e, o=yn.ap, a=yn.ap, c=gns.ap: e.tensor_tensor(o, a, c, ALU.mult),
                   reads=[yn, gns], writes=[yn])
            kk.vop(lambda e, o=yn.ap, a=yn.ap, c=gnb.ap: e.tensor_tensor(o, a, c, ALU.add),
                   reads=[yn, gnb], writes=[yn])
            kk.vop(lambda e, o=gy.ap, a=yn.ap, c=gs.ap: e.tensor_tensor(o, a, c, ALU.mult),
                   reads=[yn, gs], writes=[gy])
            kk.transpose_tile(gyT, gy, 8, ident, banks=(6, 7))
            bo2 = [(pb[4], pb[4].ap, 0), (pb[5], pb[5].ap, 512)]
            kk.gemm(bo2, gyT, wo, 8, D, 0, D)
            for hf in range(2):
                kk.vop(lambda e, o=yr[p].ap[:, hf * 512:(hf + 1) * 512],
                       a=pb[4 + hf].ap, c=xa[p].ap[:, hf * 512:(hf + 1) * 512]:
                       e.tensor_tensor(o, a, c, ALU.add),
                       reads=[pb[4 + hf], xa[p]], writes=[yr[p]])
            P.dma("ge", "go", s["SYR"][rows, :], yr[p])
            if f"SYRd{li}" in kk.s:
                P.dma("ge", "go", s[f"SYRd{li}"][rows, :], yr[p])
    P.barrier()
    pool.free()


def stage_A_S3(kk, li):
    P, s, i, pb = kk.P, kk.s, kk.i, kk.pb
    pool = Pool(kk.nc, f"s3_{li}")
    w1 = load_weight(kk, pool, "w1", i["w1a"][li], D, FF0, "w1")
    w2 = load_weight(kk, pool, "w2", i["w2a"][li], FF0, D, "w2")
    lns = pool.t("lns", [128, D]); lnb = pool.t("lnb", [128, D])
    P.dma("sy", "cs", lns, i["lnsb"]); P.dma("sy", "cs", lnb, i["lnbb"])
    ident = kk._id_c3b
    tp = kk.ln_tmp(pool)
    yr = [pool.t(f"yr{p}", [128, D]) for p in range(2)]
    x0s = [pool.t(f"x0{p}", [128, D]) for p in range(2)]
    hbuf = pool.t("h", [128, D]); hT = pool.t("hT", [128, D])
    hg = pool.t("hg", [128, FF0]); hgT = pool.t("hgT", [128, FF0])
    xn = [pool.t(f"xn{p}", [128, D]) for p in range(2)]
    lno = hbuf
    x1t = hT
    last = (li == NL0 - 1)
    for ch in range(kk.NCH):
        p = ch % 2
        rows = slice(ch * 128, (ch + 1) * 128)
        P.dma("sy", f"a{p}", yr[p], s["SYR"][rows, :])
        kk.layernorm(hbuf, yr[p], D, tp, scale=lns, bias=lnb)
        kk.transpose_tile(hT, hbuf, 8, ident, banks=(6, 7))
        bo = [(pb[k], pb[k].ap, k * 512) for k in range(4)]
        kk.gemm(bo, hT, w1, 8, FF0, 0, FF0)
        for q in range(4):
            kk.sop(lambda e, o=hg.ap[:, q * 512:(q + 1) * 512], ii=pb[q].ap:
                   e.activation(o, ii, AF.Gelu_apprx_tanh),
                   reads=[pb[q]], writes=[hg])
        kk.transpose_tile(hgT, hg, 16, ident, banks=(6, 7))
        bo2 = [(pb[4], pb[4].ap, 0), (pb[5], pb[5].ap, 512)]
        kk.gemm(bo2, hgT, w2, 16, D, 0, D)
        for hf in range(2):
            kk.vop(lambda e, o=xn[p].ap[:, hf * 512:(hf + 1) * 512],
                   a=pb[4 + hf].ap, c=yr[p].ap[:, hf * 512:(hf + 1) * 512]:
                   e.tensor_tensor(o, a, c, ALU.add),
                   reads=[pb[4 + hf], yr[p]], writes=[xn[p]])
        P.dma("ge", "go", s["XA"][rows, :], xn[p])
        if f"XAd{li}" in kk.s:
            P.dma("ge", "go", s[f"XAd{li}"][rows, :], xn[p])
        if last:
            P.dma("sy", f"b{p}", x0s[p], i["x0"][rows, :])
            kk.layernorm(lno, xn[p], D, tp)
            kk.vop(lambda e, o=x1t.ap, a=lno.ap, c=x0s[p].ap:
                   e.tensor_tensor(o, a, c, ALU.add),
                   reads=[lno, x0s[p]], writes=[x1t])
            P.dma("ge", "go", s["X1"][rows, :], x1t)
    P.barrier()
    pool.free()


def stage_B(kk):
    P, s, i, pb = kk.P, kk.s, kk.i, kk.pb
    m = kk.moe
    cap = m["cap"]
    ntl = cap // 128
    pool = Pool(kk.nc, "sb")
    tp = kk.ln_tmp(pool)
    ident = pool.t("id", [128, 128]); P.dma("sy", "cs", ident, i["ident"])
    iota = pool.t("iota", [128, 128]); P.dma("sy", "cs", iota, i["iota"])
    gi = pool.t("gi", [128, 1], I32); wi = pool.t("wi", [128, 1], I32)
    grow = pool.t("grow", [128, D])
    cwr = pool.t("cwr", [128, EMO])
    hbuf = pool.t("h", [128, D])
    hTs = [pool.t(f"hT{t}", [128, D]) for t in range(ntl)]
    cws = [pool.t(f"cw{t}", [128, EMO]) for t in range(ntl)]
    b1b = pool.t("b1b", [128, FMO]); b2b = pool.t("b2b", [128, D])
    wst = [pool.t(f"wst{p}", [128, 1024]) for p in range(2)]
    hg = [pool.t(f"hg{t}", [128, FMO]) for t in range(ntl)]
    hgT = [pool.t(f"hgT{t}", [128, FMO]) for t in range(ntl)]
    t1 = pool.t("t1", [128, D]); t2 = pool.t("t2", [128, D])
    bwrow = pool.t("bwrow", [128, D])
    for e in range(EMO):
        P.dma("sy", "cs", b1b, i["mb1"][e])
        P.dma("sy", "cs", b2b, i["mb2"][e])
        for t in range(ntl):
            P.dma("sy", "cs", gi, i["gidx"][(e * cap + t * 128):(e * cap + (t + 1) * 128), :])
            P.dma("sy", "cs", wi, i["widx"][(e * cap + t * 128):(e * cap + (t + 1) * 128), :])
            kk.P.idma("gi0", grow, gi, s["X1"])
            kk.P.idma("gi1", cws[t], wi, i["cw"])
            kk.layernorm(hbuf, grow, D, tp)
            kk.transpose_tile(hTs[t], hbuf, 8, ident, banks=(6, 7))
        for q in range(4):
            for k in range(8):
                p = k % 2
                P.dma("sy", f"w{'a' if p == 0 else 'b'}", wst[p],
                      i["mw1"][e][k * 128:(k + 1) * 128, q * 1024:(q + 1) * 1024])
                for t in range(ntl):
                    for hf in range(2):
                        bank = pb[t * 2 + hf]
                        kk.mm(bank, bank.ap, hTs[t].ap[:, k * 128:(k + 1) * 128],
                              wst[p].ap[:, hf * 512:(hf + 1) * 512],
                              start=(k == 0), stop=(k == 7),
                              reads=[hTs[t], wst[p]])
            for t in range(ntl):
                for hf in range(2):
                    bank = pb[t * 2 + hf]
                    col = q * 1024 + hf * 512
                    kk.vop(lambda e2, o=tp["xm"].ap[:, 0:512], a=bank.ap,
                           c=b1b.ap[:, col:col + 512]:
                           e2.tensor_tensor(o, a, c, ALU.add),
                           reads=[bank, b1b], writes=[tp["xm"]])
                    kk.sop(lambda e2, o=hg[t].ap[:, col:col + 512], ii=tp["xm"].ap[:, 0:512]:
                           e2.activation(o, ii, AF.Gelu_apprx_tanh),
                           reads=[tp["xm"]], writes=[hg[t]])
        for t in range(ntl):
            kk.transpose_tile(hgT[t], hg[t], 32, ident, banks=(6, 7))
        for k in range(32):
            p = k % 2
            P.dma("sy", f"w{'a' if p == 0 else 'b'}", wst[p],
                  i["mw2"][e][k * 128:(k + 1) * 128, :])
            for t in range(ntl):
                for hf in range(2):
                    bank = pb[t * 2 + hf]
                    kk.mm(bank, bank.ap, hgT[t].ap[:, k * 128:(k + 1) * 128],
                          wst[p].ap[:, hf * 512:(hf + 1) * 512],
                          start=(k == 0), stop=(k == 31),
                          reads=[hgT[t], wst[p]])
        for t in range(ntl):
            for hf in range(2):
                bank = pb[t * 2 + hf]
                kk.vop(lambda e2, o=t1.ap[:, hf * 512:(hf + 1) * 512], a=bank.ap,
                       sc=cws[t].ap[:, e:e + 1]:
                       e2.tensor_single_scalar(o, a, sc, ALU.mult),
                       reads=[bank, cws[t]], writes=[t1])
            kk.vop(lambda e2, o=t2.ap, a=b2b.ap, sc=cws[t].ap[:, e:e + 1]:
                   e2.tensor_single_scalar(o, a, sc, ALU.mult),
                   reads=[b2b, cws[t]], writes=[t2])
            kk.vop(lambda e2, o=bwrow.ap, a=t1.ap, c=t2.ap:
                   e2.tensor_tensor(o, a, c, ALU.add),
                   reads=[t1, t2], writes=[bwrow])
            P.dma("ge", "go", s["BW"][(e * cap + t * 128):(e * cap + (t + 1) * 128), :],
                  bwrow)
    P.barrier()
    si = pool.t("si", [128, 1], I32)
    so = pool.t("so", [128, 1])
    gsb = pool.t("gsb", [128, D])
    oh = pool.t("oh", [128, 128])
    x1s = pool.t("x1s", [128, D])
    bn = pool.t("bn", [128, D])
    x2t = pool.t("x2t", [128, D])
    mx = max(m["chunks_t"] + [1])
    wt = m["work_tiles"]
    done = set()
    for wi2, t in enumerate(wt):
        rows = slice(t * 128, (t + 1) * 128)
        nck = m["chunks_t"][t]
        for kq in range(nck):
            base = (wi2 * mx + kq) * 128
            P.dma("sy", "cs", si, i["sidx"][base:base + 128, :])
            P.dma("sy", "cs", so, i["soff"][base:base + 128, :])
            kk.P.idma("gi0", gsb, si, s["BW"])
            kk.vop(lambda e2, o=oh.ap, a=iota.ap, sc=so.ap:
                   e2.tensor_single_scalar(o, a, sc, ALU.is_equal),
                   reads=[iota, so], writes=[oh])
            for hf in range(2):
                bank = pb[hf]
                kk.mm(bank, bank.ap, oh.ap, gsb.ap[:, hf * 512:(hf + 1) * 512],
                      start=(kq == 0), stop=(kq == nck - 1), reads=[oh, gsb])
        P.dma("sy", "a0", x1s, s["X1"][rows, :])
        for hf in range(2):
            kk.vop(lambda e2, o=bn.ap[:, hf * 512:(hf + 1) * 512], ii=pb[hf].ap:
                   e2.tensor_copy(o, ii), reads=[pb[hf]], writes=[bn])
        kk.layernorm(t1, bn, D, tp)
        kk.vop(lambda e2, o=x2t.ap, a=t1.ap, c=x1s.ap:
               e2.tensor_tensor(o, a, c, ALU.add), reads=[t1, x1s], writes=[x2t])
        P.dma("ge", "go", s["X2"][rows, :], x2t)
        done.add(t)
    cpy = pool.t("cpy", [128, D])
    for t in range(m["ntile"]):
        if t in done:
            continue
        rows = slice(t * 128, (t + 1) * 128)
        P.dma("sy", "a1", cpy, s["X1"][rows, :])
        P.dma("ge", "go", s["X2"][rows, :], cpy)
    P.barrier()
    pool.free()


def stage_C1(kk):
    P, s, i, pb = kk.P, kk.s, kk.i, kk.pb
    Sc = kk.Sc
    pool = Pool(kk.nc, "c1")
    wx = load_weight(kk, pool, "wx", i["wxT"], D, INNER, "wx")
    wz = load_weight(kk, pool, "wz", i["wzT"], D, INNER, "wz")
    ident = pool.t("id", [128, 128]); P.dma("sy", "cs", ident, i["ident"])
    tp = kk.ln_tmp(pool)
    x2s = [pool.t(f"x2{p}", [128, D]) for p in range(2)]
    hbuf = pool.t("h", [128, D]); hT = pool.t("hT", [128, D])
    zsb = [pool.t(f"z{p}", [128, INNER]) for p in range(2)]
    xsb = pool.t("xsb", [128, INNER])
    xpT = [pool.t(f"xpT{p}", [128, INNER]) for p in range(2)]
    for b in range(B):
        for j in range(kk.JB):
            ch = b * kk.JB + j
            p = ch % 2
            rows = slice(ch * 128, (ch + 1) * 128)
            P.dma("sy", f"a{p}", x2s[p], s["X2"][rows, :])
            kk.layernorm(hbuf, x2s[p], D, tp)
            kk.transpose_tile(hT, hbuf, 8, ident, banks=(6, 7))
            bo_x = [(pb[k], pb[k].ap, k * 512) for k in range(4)]
            kk.gemm(bo_x, hT, wx, 8, INNER, 0, INNER)
            bo_z = [(pb[4 + k], pb[4 + k].ap, k * 512) for k in range(2)] + \
                   [(pb[6], pb[6].ap, 1024), (pb[7], pb[7].ap, 1536)]
            kk.gemm(bo_z, hT, wz, 8, INNER, 0, INNER)
            for q in range(4):
                kk.vop(lambda e, o=zsb[p].ap[:, q * 512:(q + 1) * 512], ii=pb[4 + q].ap:
                       e.tensor_copy(o, ii), reads=[pb[4 + q]], writes=[zsb[p]])
                kk.vop(lambda e, o=xsb.ap[:, q * 512:(q + 1) * 512], ii=pb[q].ap:
                       e.tensor_copy(o, ii), reads=[pb[q]], writes=[xsb])
            P.dma("ge", "go", s["ZP"][rows, :], zsb[p])
            kk.transpose_tile(xpT[p], xsb, 16, ident, banks=(0, 1))
            for t in range(16):
                P.dma("ge", "go",
                      s["XPT"][b * INNER + t * 128: b * INNER + (t + 1) * 128,
                               j * 128:(j + 1) * 128],
                      Buf(xpT[p].ap[:, t * 128:(t + 1) * 128], "xpTs"),
                      reads=[xpT[p]])
            if j == kk.JB - 1:
                for t in range(16):
                    P.dma("ge", "go", s["hin"][b * 16 + t:b * 16 + t + 1, :],
                          Buf(xpT[p].ap[:, t * 128 + 125:(t + 1) * 128], "halo"),
                          reads=[xpT[p]])
    kk.P.coll("AllGather", ALU.bypass, [s["hin"]], [s["hout"]], [list(range(NC))])
    P.barrier()
    pool.free()


def stage_C2(kk):
    P, s, i, pb = kk.P, kk.s, kk.i, kk.pb
    pool_keep = Pool(kk.nc, "c2k")
    accY = pool_keep.t("accY", [128, INNER])
    accQ = pool_keep.t("accQ", [128, INNER])
    ssb = pool_keep.t("ssb", [1, B * 16])
    ycs_k = [pool_keep.t(f"yc{p}", [128, INNER]) for p in range(2)]
    factb = [pool_keep.t(f"fb{b}", [128, 16]) for b in range(B)]
    pool = Pool(kk.nc, "c2")
    wb = load_weight(kk, pool, "wb", i["wbT"], INNER, H * NST, "wb")
    wc = load_weight(kk, pool, "wc", i["wcT"], INNER, H * NST, "wc")
    wdt = load_weight(kk, pool, "wdt", i["wdtT"], INNER, H, "wdt")
    ident = pool.t("id", [128, 128]); P.dma("sy", "cs", ident, i["ident"])
    tri = pool.t("tri", [128, 128]); P.dma("sy", "cs", tri, i["tri"])
    convw = pool.t("convw", [128, 16 * KERN])
    dma_kp(P, "sy", "cs", convw, i["convw"])
    convb = pool.t("convb", [128, 16])
    P.dma("sy", "cs", convb, i["convb"])
    bdt = pool.t("bdt", [H, 1]); P.dma("sy", "cs", bdt, i["bdt"])
    loga = pool.t("loga", [H, 1]); P.dma("sy", "cs", loga, i["loga"])
    dpar = pool.t("dpar", [128, H]); P.dma("sy", "cs", dpar, i["dpar"])
    ones8 = pool.t("ones8", [H, CHK]); P.dma("sy", "cs", ones8, i["ones8"])
    ones1 = pool.t("ones1", [1, 128]); P.dma("sy", "cs", ones1, i["ones_1_128"])
    ppk = pool.t("ppk", [H, 4 * 128])
    dma_kp(P, "sy", "cs", ppk, i["PP"], p=H)
    m1c = pool.t("m1c", [128, 1]); P.dma("sy", "cs", m1c, i["m1col"])
    acol = pool.t("acol", [H, 1])
    kk.sop(lambda e, o=acol.ap, ii=loga.ap: e.activation(o, ii, AF.Exp),
           reads=[loga], writes=[acol])
    kk.vop(lambda e, o=acol.ap, ii=acol.ap: e.tensor_scalar(o, ii, -1.0, None, ALU.mult),
           reads=[acol], writes=[acol])
    hsb = pool.t("hsb", [B * 16, 384])
    hix = pool.t("hix", [B * 16, 1], I32)
    P.dma("sy", "cs", hix, i["halo_i"])
    kk.P.idma("gi0", hsb, hix, s["hout"])
    S_pair = [[pool.t(f"S{b}_{q}", [128, HD2]) for q in range(H // 2)]
              for b in range(B)]
    llc = [pool.t(f"llc{b}", [H, 1]) for b in range(B)]
    work = [pool.t(f"wk{t}", [128, 131]) for t in range(16)]
    xcT = pool.t("xcT", [128, 16 * 128])
    xc = pool.t("xc", [128, INNER])
    dts = pool.t("dts", [H, CHK])
    LT = pool.t("LT", [H, CHK])
    eL = pool.t("eL", [H, CHK]); enL = pool.t("enL", [H, CHK])
    fT = pool.t("fT", [H, CHK])
    fcol = pool.t("fcol", [128, H]); gcol2 = pool.t("gcol2", [128, H])
    bf = pool.t("bf", [128, H * NST]); cf = pool.t("cf", [128, H * NST])
    bfT = pool.t("bfT", [128, H * NST]); cfT = pool.t("cfT", [128, H * NST])
    ctg = pool.t("ctg", [128, H * NST])
    adt = pool.t("adt", [H, CHK])
    xs_h = pool.t("xs_h", [128, HD2])
    atts = pool.t("atts", [128, 128])
    ych = pool.t("ych", [128, INNER])
    tdp = pool.t("tdp", [128, HD2])
    exl = pool.t("exl", [H, 1])
    ccol = pool.t("ccol", [128, 1])
    erow = pool.t("erow", [1, H]); erowE = pool.t("erowE", [1, H])
    ecol = pool.t("ecol", [NST, H])
    st_t = pool.t("st_t", [NST, HD2])
    tp = {"xm": pool.t("cv1", [128, 128]), "sq": pool.t("cv2", [128, 128])}
    for b in range(B):
        for h in range(H):
            if h % 2 == 0:
                P.op("ve", lambda e, o=S_pair[b][h // 2].ap: e.memset(o, 0.0),
                     writes=[S_pair[b][h // 2]])
        P.op("ve", lambda e, o=llc[b].ap: e.memset(o, 0.0), writes=[llc[b]])
        for j in range(kk.JB):
            ch = b * kk.JB + j
            rows = slice(ch * 128, (ch + 1) * 128)
            for t in range(16):
                rt = slice(b * INNER + t * 128, b * INNER + (t + 1) * 128)
                if j > 0:
                    P.dma("sy", "a0" if t % 2 == 0 else "a1", work[t],
                          s["XPT"][rt, j * 128 - 3: (j + 1) * 128])
                else:
                    P.dma("sy", "a0" if t % 2 == 0 else "a1",
                          Buf(work[t].ap[:, 3:131], "wkx"),
                          s["XPT"][rt, 0:128], writes=[work[t]])
            if j == 0:
                for t in range(16):
                    P.dma("ge", "gx0", Buf(work[t].ap[:, 0:3], "wkh2"),
                          Buf(hsb.ap[b * 16 + t:b * 16 + t + 1, :], "hrow"),
                          reads=[hsb], writes=[work[t]])
                    kk.vop(lambda e, o=work[t].ap[:, 0:3], ii=work[t].ap[:, 0:3],
                           sc=m1c.ap:
                           e.tensor_single_scalar(o, ii, sc, ALU.mult),
                           reads=[work[t], m1c], writes=[work[t]])
            for t in range(16):
                acc = tp["xm"].ap[:, 0:128]
                kk.vop(lambda e, o=acc, ii=work[t].ap[:, 0:128],
                       sc=convw.ap[:, t * KERN:t * KERN + 1]:
                       e.tensor_single_scalar(o, ii, sc, ALU.mult),
                       reads=[work[t], convw], writes=[tp["xm"]])
                for kq in range(1, KERN):
                    kk.vop(lambda e, o=tp["sq"].ap[:, 0:128], ii=work[t].ap[:, kq:kq + 128],
                           sc=convw.ap[:, t * KERN + kq:t * KERN + kq + 1]:
                           e.tensor_single_scalar(o, ii, sc, ALU.mult),
                           reads=[work[t], convw], writes=[tp["sq"]])
                    kk.vop(lambda e, o=acc, a=acc, c=tp["sq"].ap[:, 0:128]:
                           e.tensor_tensor(o, a, c, ALU.add),
                           reads=[tp["sq"], tp["xm"]], writes=[tp["xm"]])
                kk.vop(lambda e, o=acc, ii=acc, sc=convb.ap[:, t:t + 1]:
                       e.tensor_single_scalar(o, ii, sc, ALU.add),
                       reads=[tp["xm"], convb], writes=[tp["xm"]])
                kk.sop(lambda e, o=xcT.ap[:, t * 128:(t + 1) * 128], ii=acc:
                       e.activation(o, ii, AF.Silu), reads=[tp["xm"]], writes=[xcT])
            bdt_bank = pb[0]
            for k in range(16):
                kk.mm(bdt_bank, bdt_bank.ap[0:H, 0:CHK],
                      wdt.ap[:, k * H:(k + 1) * H], xcT.ap[:, k * 128:(k + 1) * 128],
                      start=(k == 0), stop=(k == 15), reads=[wdt, xcT])
            kk.vop(lambda e, o=dts.ap, ii=bdt_bank.ap[0:H, 0:CHK], sc=bdt.ap:
                   e.tensor_single_scalar(o, ii, sc, ALU.add),
                   reads=[bdt_bank, bdt], writes=[dts])
            kk.sop(lambda e, o=eL.ap, ii=dts.ap: e.activation(o, ii, AF.Abs),
                   reads=[dts], writes=[eL])
            kk.sop(lambda e, o=eL.ap, ii=eL.ap: e.activation(o, ii, AF.Exp, scale=-1.0),
                   reads=[eL], writes=[eL])
            kk.vop(lambda e, o=eL.ap, ii=eL.ap: e.tensor_scalar(o, ii, 1.0, None, ALU.add),
                   reads=[eL], writes=[eL])
            kk.sop(lambda e, o=eL.ap, ii=eL.ap: e.activation(o, ii, AF.Ln),
                   reads=[eL], writes=[eL])
            kk.vop(lambda e, o=enL.ap, ii=dts.ap: e.tensor_scalar(o, ii, 0.0, None, ALU.max),
                   reads=[dts], writes=[enL])
            kk.vop(lambda e, o=dts.ap, a=eL.ap, c=enL.ap: e.tensor_tensor(o, a, c, ALU.add),
                   reads=[eL, enL], writes=[dts])
            kk.vop(lambda e, o=adt.ap, ii=dts.ap, sc=acol.ap:
                   e.tensor_single_scalar(o, ii, sc, ALU.mult),
                   reads=[dts, acol], writes=[adt])
            kk.vop(lambda e, o=LT.ap, d0=ones8.ap, d1=adt.ap:
                   e.tensor_tensor_scan(o, d0, d1, 0.0, ALU.mult, ALU.add),
                   reads=[ones8, adt], writes=[LT])
            kk.sop(lambda e, o=eL.ap, ii=LT.ap: e.activation(o, ii, AF.Exp),
                   reads=[LT], writes=[eL])
            kk.sop(lambda e, o=enL.ap, ii=LT.ap: e.activation(o, ii, AF.Exp, scale=-1.0),
                   reads=[LT], writes=[enL])
            kk.vop(lambda e, o=fT.ap, a=enL.ap, c=dts.ap: e.tensor_tensor(o, a, c, ALU.mult),
                   reads=[enL, dts], writes=[fT])
            btr = pb[1]
            kk.transpose(btr, btr.ap[0:CHK, 0:H], fT.ap, ident.ap[0:H, 0:H],
                         reads=[fT, ident])
            kk.vop(lambda e, o=fcol.ap, ii=btr.ap[0:CHK, 0:H]: e.tensor_copy(o, ii),
                   reads=[btr], writes=[fcol])
            kk.transpose(btr, btr.ap[0:CHK, 0:H], eL.ap, ident.ap[0:H, 0:H],
                         reads=[eL, ident])
            kk.vop(lambda e, o=gcol2.ap, ii=btr.ap[0:CHK, 0:H]: e.tensor_copy(o, ii),
                   reads=[btr], writes=[gcol2])
            bB = pb[2]
            for k in range(16):
                kk.mm(bB, bB.ap, xcT.ap[:, k * 128:(k + 1) * 128],
                      wb.ap[:, k * (H * NST):(k + 1) * (H * NST)],
                      start=(k == 0), stop=(k == 15), reads=[xcT, wb])
            kk.vop(lambda e, o=bf.ap, ii=bB.ap: e.tensor_copy(o, ii),
                   reads=[bB], writes=[bf])
            bC = pb[3]
            for k in range(16):
                kk.mm(bC, bC.ap, xcT.ap[:, k * 128:(k + 1) * 128],
                      wc.ap[:, k * (H * NST):(k + 1) * (H * NST)],
                      start=(k == 0), stop=(k == 15), reads=[xcT, wc])
            for h in range(H):
                kk.vop(lambda e, o=cf.ap[:, h * NST:(h + 1) * NST],
                       ii=bC.ap[:, h * NST:(h + 1) * NST], sc=gcol2.ap[:, h:h + 1]:
                       e.tensor_single_scalar(o, ii, sc, ALU.mult),
                       reads=[bC, gcol2], writes=[cf])
            kk.transpose_tile(bfT, bf, 4, ident, banks=(4, 5))
            kk.transpose_tile(cfT, cf, 4, ident, banks=(4, 5))
            kk.transpose_tile(xc, xcT, 16, ident, banks=(4, 5))
            kk.sop(lambda e, o=exl.ap, ii=llc[b].ap: e.activation(o, ii, AF.Exp),
                   reads=[llc[b]], writes=[exl])
            for pq in range(4):
                bk = pb[6]
                kk.mm(bk, bk.ap[:, 0:1], ppk.ap[:, pq * 128:(pq + 1) * 128],
                      exl.ap, True, True, reads=[ppk, exl])
                kk.vop(lambda e, o=ccol.ap, ii=bk.ap[:, 0:1]: e.tensor_copy(o, ii),
                       reads=[bk], writes=[ccol])
                kk.vop(lambda e, o=ctg.ap[:, pq * 128:(pq + 1) * 128],
                       ii=cfT.ap[:, pq * 128:(pq + 1) * 128], sc=ccol.ap:
                       e.tensor_single_scalar(o, ii, sc, ALU.mult),
                       reads=[cfT, ccol], writes=[ctg])
                P.dma("ge", "go",
                      s["CFT"][ch * 512 + pq * 128: ch * 512 + (pq + 1) * 128, :],
                      Buf(ctg.ap[:, pq * 128:(pq + 1) * 128], "ctgs"), reads=[ctg])
            bk = pb[6]
            kk.mm(bk, bk.ap[0:1, 0:H], LT.ap[:, 127:128],
                  ident.ap[0:H, 0:H], True, True, reads=[LT, ident])
            kk.vop(lambda e, o=erow.ap, ii=bk.ap[0:1, 0:H]: e.tensor_copy(o, ii),
                   reads=[bk], writes=[erow])
            kk.sop(lambda e, o=erowE.ap, ii=erow.ap: e.activation(o, ii, AF.Exp),
                   reads=[erow], writes=[erowE])
            kk.mm(bk, bk.ap[0:NST, 0:H], ones1.ap[:, 0:NST], erowE.ap, True, True,
                  reads=[ones1, erowE])
            kk.vop(lambda e, o=ecol.ap, ii=bk.ap[0:NST, 0:H]: e.tensor_copy(o, ii),
                   reads=[bk], writes=[ecol])
            for h in range(H):
                pq, half = h // 2, h % 2
                kk.vop(lambda e, o=xs_h.ap, ii=xc.ap[:, h * HD2:(h + 1) * HD2],
                       sc=fcol.ap[:, h:h + 1]:
                       e.tensor_single_scalar(o, ii, sc, ALU.mult),
                       reads=[xc, fcol], writes=[xs_h])
                ba = pb[6] if h % 2 == 0 else pb[7]
                kk.mm(ba, ba.ap[:, 0:128],
                      bfT.ap[half * 64:half * 64 + 64, pq * 128:(pq + 1) * 128],
                      cfT.ap[half * 64:half * 64 + 64, pq * 128:(pq + 1) * 128],
                      True, True, reads=[bfT, cfT])
                kk.vop(lambda e, o=atts.ap, a=ba.ap[:, 0:128], t=tri.ap:
                       e.tensor_tensor(o, a, t, ALU.mult), reads=[ba, tri], writes=[atts])
                by = pb[4] if h % 2 == 0 else pb[5]
                kk.mm(by, by.ap[:, 0:HD2], atts.ap, xs_h.ap, True, False,
                      reads=[atts, xs_h])
                kk.mm(by, by.ap[:, 0:HD2],
                      cfT.ap[half * 64:half * 64 + 64, pq * 128:(pq + 1) * 128],
                      S_pair[b][pq].ap[half * 64:half * 64 + 64, :],
                      False, True, reads=[cfT, S_pair[b][pq]])
                kk.vop(lambda e, o=tdp.ap, ii=xc.ap[:, h * HD2:(h + 1) * HD2],
                       sc=dpar.ap[:, h:h + 1]:
                       e.tensor_single_scalar(o, ii, sc, ALU.mult),
                       reads=[xc, dpar], writes=[tdp])
                kk.vop(lambda e, o=ych.ap[:, h * HD2:(h + 1) * HD2],
                       a=by.ap[:, 0:HD2], c=tdp.ap:
                       e.tensor_tensor(o, a, c, ALU.add),
                       reads=[by, tdp], writes=[ych])
                bs_ = pb[6] if h % 2 == 0 else pb[7]
                kk.mm(bs_, bs_.ap[0:NST, 0:HD2], bf.ap[:, h * NST:(h + 1) * NST],
                      xs_h.ap, True, True, reads=[bf, xs_h])
                kk.vop(lambda e, o=st_t.ap,
                       a=S_pair[b][pq].ap[half * 64:half * 64 + 64, :],
                       c=bs_.ap[0:NST, 0:HD2]:
                       e.tensor_tensor(o, a, c, ALU.add),
                       reads=[S_pair[b][pq], bs_], writes=[st_t])
                kk.vop(lambda e, o=S_pair[b][pq].ap[half * 64:half * 64 + 64, :],
                       ii=st_t.ap, sc=ecol.ap[:, h:h + 1]:
                       e.tensor_single_scalar(o, ii, sc, ALU.mult),
                       reads=[st_t, ecol], writes=[S_pair[b][pq]])
            P.dma("ge", "go", s["YC"][rows, :], ych)
            kk.vop(lambda e, o=llc[b].ap, a=llc[b].ap, c=LT.ap[:, 127:128]:
                   e.tensor_tensor(o, a, c, ALU.add), reads=[LT, llc[b]], writes=[llc[b]])
        for h in range(H):
            P.dma("ge", "go", s["ssin"][(b * H + h) * NST:(b * H + h + 1) * NST, :],
                  Buf(S_pair[b][h // 2].ap[(h % 2) * 64:(h % 2) * 64 + 64, :], "sst"),
                  reads=[S_pair[b][h // 2]])
        P.dma("ge", "go", s["llin"][b * H:(b + 1) * H, :], Buf(llc[b].ap, "llv"),
              reads=[llc[b]])
    kk.P.coll("AllGather", ALU.bypass, [s["ssin"]], [s["ssout"]], [list(range(NC))])
    kk.P.coll("AllGather", ALU.bypass, [s["llin"]], [s["llout"]], [list(range(NC))])
    P.barrier()
    pool.free()
    kk._c2_keep = (pool_keep, accY, accQ, ssb, ycs_k, factb)


def stage_C3(kk):
    P, s, i, pb = kk.P, kk.s, kk.i, kk.pb
    pool_keep, accY, accQ, ssb, ycs, factb = kk._c2_keep
    pool = Pool(kk.nc, "c3")
    ident = pool.t("id", [128, 128]); P.dma("sy", "cs", ident, i["ident"])
    ones1 = pool.t("ones1", [1, 128]); P.dma("sy", "cs", ones1, i["ones_1_128"])
    onec = pool.t("onec", [128, 1])
    P.op("ve", lambda e, o=onec.ap: e.memset(o, 1.0), writes=[onec])
    qq = pool.t("qq", [B * H, 16 * 128])
    dma_kp(P, "sy", "cs", qq, i["QQ"], p=B * H)
    m1c = pool.t("m1c", [128, 1]); P.dma("sy", "cs", m1c, i["m1col"])
    m2c = pool.t("m2c", [128, 1]); P.dma("sy", "cs", m2c, i["m2col"])
    sp1 = [pool.t(f"sp1_{k}", [128, HD2]) for k in range(B * H // 2)]
    sp2 = [pool.t(f"sp2_{k}", [128, HD2]) for k in range(B * H // 2)]
    six = pool.t("six", [128, 1], I32)
    for k in range(B * H // 2):
        P.dma("sy", "cs", six, i["ssd_i1"][k * 128:(k + 1) * 128, :])
        kk.P.idma("gi0", sp1[k], six, s["ssout"])
        P.dma("sy", "cs", six, i["ssd_i2"][k * 128:(k + 1) * 128, :])
        kk.P.idma("gi0", sp2[k], six, s["ssout"])
    llx = pool.t("llx", [B * H, 1], I32)
    llp = pool.t("llp", [B * H, 1])
    P.dma("sy", "cs", llx, i["ll_i1"])
    kk.P.idma("gi1", llp, llx, s["llout"])
    exl2 = pool.t("exl2", [B * H, 1])
    kk.sop(lambda e, o=exl2.ap, ii=llp.ap: e.activation(o, ii, AF.Exp),
           reads=[llp], writes=[exl2])
    w2c = pool.t("w2c", [128, 1])
    sin_t = [pool.t(f"sint{k}", [128, HD2]) for k in range(B * H // 2)]
    tq = pool.t("tq", [128, HD2])
    for k in range(B * H // 2):
        bk = pb[6]
        kk.mm(bk, bk.ap[:, 0:1], qq.ap[:, k * 128:(k + 1) * 128], exl2.ap,
              True, True, reads=[qq, exl2])
        kk.vop(lambda e, o=w2c.ap, ii=bk.ap[:, 0:1]: e.tensor_copy(o, ii),
               reads=[bk], writes=[w2c])
        kk.vop(lambda e, o=w2c.ap, a=w2c.ap, c=m2c.ap: e.tensor_tensor(o, a, c, ALU.mult),
               reads=[w2c, m2c], writes=[w2c])
        kk.vop(lambda e, o=tq.ap, ii=sp2[k].ap, sc=w2c.ap:
               e.tensor_single_scalar(o, ii, sc, ALU.mult),
               reads=[sp2[k], w2c], writes=[tq])
        kk.vop(lambda e, o=sin_t[k].ap, ii=sp1[k].ap, sc=m1c.ap:
               e.tensor_single_scalar(o, ii, sc, ALU.mult),
               reads=[sp1[k], m1c], writes=[sin_t[k]])
        kk.vop(lambda e, o=sin_t[k].ap, a=sin_t[k].ap, c=tq.ap:
               e.tensor_tensor(o, a, c, ALU.add), reads=[tq, sin_t[k]], writes=[sin_t[k]])
    cft = [pool.t(f"cft{p}", [128, 512]) for p in range(2)]
    yft = [pool.t(f"yf{p}", [128, INNER]) for p in range(2)]
    sq2 = pool.t("sq2", [128, INNER])
    for b in range(B):
        P.op("ve", lambda e, o=accY.ap: e.memset(o, 0.0), writes=[accY])
        P.op("ve", lambda e, o=accQ.ap: e.memset(o, 0.0), writes=[accQ])
        for j in range(kk.JB):
            ch = b * kk.JB + j
            p = ch % 2
            rows = slice(ch * 128, (ch + 1) * 128)
            dma_kp(P, "sy", f"a{p}", cft[p], s["CFT"][ch * 512:(ch + 1) * 512, :])
            P.dma("sy", f"b{p}", ycs[p], s["YC"][rows, :])
            for h in range(H):
                pq, half = h // 2, h % 2
                k = (b * H + h) // 2
                shalf = (b * H + h) % 2
                by = pb[4] if h % 2 == 0 else pb[5]
                kk.mm(by, by.ap[:, 0:HD2],
                      cft[p].ap[half * 64:half * 64 + 64, pq * 128:(pq + 1) * 128],
                      sin_t[k].ap[shalf * 64:shalf * 64 + 64, :],
                      True, True, reads=[cft[p], sin_t[k]])
                kk.vop(lambda e, o=yft[p].ap[:, h * HD2:(h + 1) * HD2],
                       a=by.ap[:, 0:HD2], c=ycs[p].ap[:, h * HD2:(h + 1) * HD2]:
                       e.tensor_tensor(o, a, c, ALU.add),
                       reads=[by, ycs[p]], writes=[yft[p]])
            P.dma("ge", "go", s["YC"][rows, :], yft[p])
            kk.vop(lambda e, o=accY.ap, a=accY.ap, c=yft[p].ap:
                   e.tensor_tensor(o, a, c, ALU.add), reads=[yft[p], accY], writes=[accY])
            kk.sop(lambda e, o=sq2.ap, ii=yft[p].ap: e.activation(o, ii, AF.Square),
                   reads=[yft[p]], writes=[sq2])
            kk.vop(lambda e, o=accQ.ap, a=accQ.ap, c=sq2.ap:
                   e.tensor_tensor(o, a, c, ALU.add), reads=[sq2, accQ], writes=[accQ])
        for q in range(4):
            bk = pb[q]
            kk.mm(bk, bk.ap[0:1, 0:512], onec.ap, accY.ap[:, q * 512:(q + 1) * 512],
                  True, True, reads=[onec, accY])
            w = bk.ap[0:1, 0:512].rearrange("p (g w) -> p g w", g=2)
            kk.vop(lambda e, o=ssb.ap[0:1, b * 16 + q * 2:b * 16 + (q + 1) * 2], ii=w:
                   e.tensor_reduce(o, ii, AX.X, ALU.add), reads=[bk], writes=[ssb])
            bk2 = pb[4 + q]
            kk.mm(bk2, bk2.ap[0:1, 0:512], onec.ap, accQ.ap[:, q * 512:(q + 1) * 512],
                  True, True, reads=[onec, accQ])
            w2 = bk2.ap[0:1, 0:512].rearrange("p (g w) -> p g w", g=2)
            kk.vop(lambda e, o=ssb.ap[0:1, b * 16 + 8 + q * 2:b * 16 + 8 + (q + 1) * 2],
                   ii=w2:
                   e.tensor_reduce(o, ii, AX.X, ALU.add), reads=[bk2], writes=[ssb])
    P.dma("ge", "go", s["srin"], ssb)
    kk.P.coll("AllReduce", ALU.add, [s["srin"]], [s["srout"]], [list(range(NC))])
    sss = pool.t("sss", [1, B * 16])
    P.raw_wait("ge", Ev("gc", P.cnt["gc"]))
    P.dma("ge", "gx0", sss, s["srout"])
    N2 = float(kk.S * HD2)
    fct = pool.t("fct", [1, B * 16])
    sv = sss.ap.rearrange("p (b t) -> p b t", b=B)
    fv = fct.ap.rearrange("p (b t) -> p b t", b=B)
    mtmp = pool.t("mtmp", [1, B * 8])
    vtmp = pool.t("vtmp", [1, B * 8])
    mv = mtmp.ap.rearrange("p (b t) -> p b t", b=B)
    vv = vtmp.ap.rearrange("p (b t) -> p b t", b=B)
    kk.vop(lambda e, o=fv[:, :, 0:8], ii=sv[:, :, 0:8]:
           e.tensor_scalar(o, ii, -1.0 / N2, None, ALU.mult), reads=[sss], writes=[fct])
    kk.vop(lambda e, o=mv, a=fv[:, :, 0:8], c=fv[:, :, 0:8]:
           e.tensor_tensor(o, a, c, ALU.mult), reads=[fct], writes=[mtmp])
    kk.vop(lambda e, o=vv, ii=sv[:, :, 8:16]:
           e.tensor_scalar(o, ii, 1.0 / N2, None, ALU.mult), reads=[sss], writes=[vtmp])
    kk.vop(lambda e, o=vtmp.ap, a=vtmp.ap, c=mtmp.ap:
           e.tensor_tensor(o, a, c, ALU.subtract), reads=[vtmp, mtmp], writes=[vtmp])
    kk.vop(lambda e, o=mv, ii=vv: e.tensor_scalar(o, ii, EPS, None, ALU.add),
           reads=[vtmp], writes=[mtmp])
    kk.sop(lambda e, o=mv, ii=mv: e.activation(o, ii, AF.Sqrt),
           reads=[mtmp], writes=[mtmp])
    kk.vop(lambda e, o=fv[:, :, 8:16], ii=mv: e.reciprocal(o, ii),
           reads=[mtmp], writes=[fct])
    for b in range(B):
        bk = pb[6]
        kk.mm(bk, bk.ap[:, 0:16], ones1.ap, fct.ap[0:1, b * 16:(b + 1) * 16],
              True, True, reads=[ones1, fct])
        kk.vop(lambda e, o=factb[b].ap, ii=bk.ap[:, 0:16]: e.tensor_copy(o, ii),
               reads=[bk], writes=[factb[b]])
    P.barrier()
    pool.free()
    pool2 = Pool(kk.nc, "c3b")
    wout = load_weight(kk, pool2, "wout", i["woutT"], INNER, D, "wout")
    gnw = pool2.t("gnw", [128, INNER]); P.dma("sy", "cs", gnw, i["gnw2"])
    gnbb = pool2.t("gnbb", [128, INNER]); P.dma("sy", "cs", gnbb, i["gnb2"])
    tp = kk.ln_tmp(pool2)
    zps = [pool2.t(f"zp{p}", [128, INNER]) for p in range(2)]
    x2s = [pool2.t(f"x2{p}", [128, D]) for p in range(2)]
    yn = pool2.t("yn", [128, INNER])
    zs = pool2.t("zs", [128, INNER])
    gy = pool2.t("gy", [128, INNER])
    gyT = pool2.t("gyT", [128, INNER])
    cn = pool2.t("cn", [128, D])
    x3t = pool2.t("x3t", [128, D])
    for b in range(B):
        for j in range(kk.JB):
            ch = b * kk.JB + j
            p = ch % 2
            rows = slice(ch * 128, (ch + 1) * 128)
            yfb = ycs[p]
            P.dma("sy", f"a{p}", yfb, s["YC"][rows, :])
            P.dma("sy", f"b{p}", zps[p], s["ZP"][rows, :])
            P.dma("sy", f"c{p}", x2s[p], s["X2"][rows, :])
            for h in range(H):
                hs = slice(h * HD2, (h + 1) * HD2)
                kk.ts2(yn, yn.ap[:, hs], yfb, yfb.ap[:, hs],
                       factb[b], factb[b].ap[:, h:h + 1],
                       factb[b].ap[:, 8 + h:9 + h])
            kk.vop(lambda e, o=yn.ap, a=yn.ap, c=gnw.ap: e.tensor_tensor(o, a, c, ALU.mult),
                   reads=[yn, gnw], writes=[yn])
            kk.vop(lambda e, o=yn.ap, a=yn.ap, c=gnbb.ap: e.tensor_tensor(o, a, c, ALU.add),
                   reads=[yn, gnbb], writes=[yn])
            kk.sop(lambda e, o=zs.ap, ii=zps[p].ap: e.activation(o, ii, AF.Silu),
                   reads=[zps[p]], writes=[zs])
            kk.vop(lambda e, o=gy.ap, a=yn.ap, c=zs.ap: e.tensor_tensor(o, a, c, ALU.mult),
                   reads=[yn, zs], writes=[gy])
            kk.transpose_tile(gyT, gy, 16, kk._id_c3b, banks=(6, 7))
            bo = [(pb[0], pb[0].ap, 0), (pb[1], pb[1].ap, 512)]
            kk.gemm(bo, gyT, wout, 16, D, 0, D)
            bn = kk._bn_c3b
            for hf in range(2):
                kk.vop(lambda e, o=bn.ap[:, hf * 512:(hf + 1) * 512], ii=pb[hf].ap:
                       e.tensor_copy(o, ii), reads=[pb[hf]], writes=[bn])
            kk.layernorm(cn, bn, D, tp)
            kk.vop(lambda e, o=x3t.ap, a=cn.ap, c=x2s[p].ap:
                   e.tensor_tensor(o, a, c, ALU.add), reads=[cn, x2s[p]], writes=[x3t])
            P.dma("ge", "go", s["X3"][rows, :], x3t)
    P.barrier()
    pool2.free()
    pool_keep.free()


def stage_D(kk):
    P, s, i, pb = kk.P, kk.s, kk.i, kk.pb
    pool = Pool(kk.nc, "sd")
    identbf = pool.t("idbf", [128, 128], BF16)
    P.dma("ge", "gx0", identbf, i["identbf"])
    tp = {"red": pool.t("red", [128, 3]), "xm": pool.t("xm", [128, D]),
          "sq": pool.t("sq", [128, D]), "rst": pool.t("rst", [128, 1])}
    w1q = pool.t("w1q", [128, 8 * FFD], BF16)
    w2q = pool.t("w2q", [128, 32 * D], BF16)
    wtmp = pool.t("wt", [128, FFD])

    def quant_tile(dst_ap, src_ap, w, scale, rbufs, wbufs):
        for c0 in range(0, w, 1024):
            ww = min(1024, w - c0)
            kk.vop(lambda e, o=tp["xm"].ap[:, :ww], ii=src_ap[:, c0:c0 + ww]:
                   e.tensor_scalar(o, ii, scale, RC, ALU.mult, ALU.add),
                   reads=rbufs, writes=[tp["xm"]])
            kk.vop(lambda e, o=tp["xm"].ap[:, :ww], ii=tp["xm"].ap[:, :ww]:
                   e.tensor_scalar(o, ii, -RC, -128.0, ALU.add, ALU.max),
                   reads=[tp["xm"]], writes=[tp["xm"]])
            kk.vop(lambda e, o=dst_ap[:, c0:c0 + ww], ii=tp["xm"].ap[:, :ww],
                   isc=1.0 / scale:
                   e.tensor_scalar(o, ii, 127.0, isc, ALU.min, ALU.mult),
                   reads=[tp["xm"]], writes=wbufs)

    for k in range(8):
        P.dma("sy", "wa" if k % 2 == 0 else "wb", wtmp,
              i["dw1T"][k * 128:(k + 1) * 128, :])
        quant_tile(w1q.ap[:, k * FFD:(k + 1) * FFD], wtmp.ap, FFD, 64.0,
                   [wtmp], [w1q])
    for k in range(32):
        P.dma("sy", "wa" if k % 2 == 0 else "wb",
              Buf(wtmp.ap[:, 0:D], "w2l"), i["dw2T"][k * 128:(k + 1) * 128, :],
              writes=[wtmp])
        quant_tile(w2q.ap[:, k * D:(k + 1) * D], wtmp.ap[:, 0:D], D, 64.0,
                   [wtmp], [w2q])
    b1q = pool.t("b1q", [128, FFD]); b2q = pool.t("b2q", [128, D])
    b1r = wtmp
    P.dma("sy", "cs", b1r, i["db1"])
    quant_tile(b1q.ap, b1r.ap, FFD, 64.0, [b1r], [b1q])
    b2r = wtmp
    P.dma("sy", "cs", Buf(b2r.ap[:, 0:D], "b2l"), i["db2"], writes=[b2r])
    quant_tile(b2q.ap, b2r.ap[:, 0:D], D, 64.0, [b2r], [b2q])
    x3s = [pool.t(f"x3{p}", [128, D]) for p in range(2)]
    xq = pool.t("xq", [128, D], BF16)
    xqT = pool.t("xqT", [128, D], BF16)
    hq = pool.t("hq", [128, FFD], BF16)
    hqT = pool.t("hqT", [128, FFD], BF16)
    dout = Buf(wtmp.ap[:, 0:D], "dout")
    lno = Buf(wtmp.ap[:, D:2 * D], "lno")
    for ch in range(kk.NCH):
        p = ch % 2
        rows = slice(ch * 128, (ch + 1) * 128)
        P.dma("sy", f"a{p}", x3s[p], s["X3"][rows, :])
        kk.layernorm(tp["sq"], x3s[p], D, tp)
        quant_tile(xq.ap, tp["sq"].ap, D, 16.0, [tp["sq"]], [xq])
        if STOP_AFTER == "Dq":
            kk.vop(lambda e, o=dout.ap, ii=xq.ap: e.tensor_copy(o, ii),
                   reads=[xq], writes=[dout])
            P.dma("ge", "go", kk.y[rows, :], dout)
            continue
        kk.transpose_tile(xqT, xq, 8, identbf, banks=(6, 7))
        for half in range(2):
            bo = [(pb[half * 4 + q], pb[half * 4 + q].ap, 0) for q in range(4)]
            for k in range(8):
                l = xqT.ap[:, k * 128:(k + 1) * 128]
                for q in range(4):
                    bank = pb[half * 4 + q]
                    col = k * FFD + half * 2048 + q * 512
                    kk.mm(bank, bank.ap, l, w1q.ap[:, col:col + 512],
                          start=(k == 0), stop=(k == 7), reads=[xqT, w1q])
            for q in range(4):
                bank = pb[half * 4 + q]
                col = half * 2048 + q * 512
                kk.vop(lambda e, o=tp["sq"].ap[:, 0:512], a=bank.ap,
                       c=b1q.ap[:, col:col + 512]:
                       e.tensor_tensor(o, a, c, ALU.add),
                       reads=[bank, b1q], writes=[tp["sq"]])
                kk.vop(lambda e, o=tp["xm"].ap[:, 0:512], ii=tp["sq"].ap[:, 0:512]:
                       e.tensor_scalar(o, ii, 16.0, RC, ALU.mult, ALU.add),
                       reads=[tp["sq"]], writes=[tp["xm"]])
                kk.vop(lambda e, o=tp["xm"].ap[:, 0:512], ii=tp["xm"].ap[:, 0:512]:
                       e.tensor_scalar(o, ii, -RC, 0.0, ALU.add, ALU.max),
                       reads=[tp["xm"]], writes=[tp["xm"]])
                kk.vop(lambda e, o=hq.ap[:, col:col + 512], ii=tp["xm"].ap[:, 0:512]:
                       e.tensor_scalar(o, ii, 127.0, 0.0625, ALU.min, ALU.mult),
                       reads=[tp["xm"]], writes=[hq])
        kk.transpose_tile(hqT, hq, 32, identbf, banks=(6, 7))
        bo2 = [(pb[0], pb[0].ap, 0), (pb[1], pb[1].ap, 512)]
        for k in range(32):
            l = hqT.ap[:, k * 128:(k + 1) * 128]
            for (bank, pap, cc) in bo2:
                kk.mm(bank, pap, l, w2q.ap[:, k * D + cc:k * D + cc + 512],
                      start=(k == 0), stop=(k == 31), reads=[hqT, w2q])
        for hf in range(2):
            kk.vop(lambda e, o=dout.ap[:, hf * 512:(hf + 1) * 512], a=pb[hf].ap,
                   c=b2q.ap[:, hf * 512:(hf + 1) * 512]:
                   e.tensor_tensor(o, a, c, ALU.add),
                   reads=[pb[hf], b2q], writes=[dout])
        kk.layernorm(lno, dout, D, tp)
        kk.vop(lambda e, o=x3s[p].ap, a=lno.ap, c=x3s[p].ap:
               e.tensor_tensor(o, a, c, ALU.add), reads=[lno, x3s[p]], writes=[x3s[p]])
        P.dma("ge", "go", kk.y[rows, :], x3s[p])
    P.barrier()
    pool.free()



def build(S, moe):
    kk = K(S, moe)
    pers = Pool(kk.nc, "pers")
    kk._id_c3b = pers.t("idc", [128, 128])
    kk.P.dma("sy", "cs", kk._id_c3b, kk.i["ident"])
    kk._bn_c3b = pers.t("bnc", [128, D])
    def _stop(name, which):
        if STOP_AFTER != name:
            return False
        pool = Pool(kk.nc, "stop")
        t = pool.t("t", [128, D])
        for ch in range(kk.NCH):
            rows = slice(ch * 128, (ch + 1) * 128)
            kk.P.dma("sy", "a0", t, kk.s[which][rows, 0:D]
                     if kk.s[which].shape[1] > D else kk.s[which][rows, :])
            kk.P.dma("ge", "go", kk.y[rows, :], t)
        kk.P.barrier()
        pool.free()
        return True

    stage_P0(kk, None)
    done = _stop("P0", "XA")
    if not done:
        for li in range(NL0):
            stage_A_S1(kk, li)
            stage_A_S2(kk, li, None)
            stage_A_S3(kk, li)
        done = _stop("A", "X1")
    if not done:
        stage_B(kk)
        done = _stop("B", "X2")
    if not done:
        stage_C1(kk)
        done = _stop("C1", "ZP")
    if not done:
        stage_C2(kk)
        done = _stop("C2", "YC")
    if not done:
        stage_C3(kk)
        done = _stop("C", "X3")
    if not done:
        stage_D(kk)
    kk.P.barrier()
    nc = kk.nc
    P = kk.P
    with nc.Block() as blk:
        @blk.tensor
        def _(e):
            for f in P.ops["te"]:
                f(e)

        @blk.vector
        def _(e):
            for f in P.ops["ve"]:
                f(e)

        @blk.scalar
        def _(e):
            for f in P.ops["se"]:
                f(e)

        @blk.gpsimd
        def _(e):
            for f in P.ops["ge"]:
                f(e)

        @blk.sync
        def _(e):
            for f in P.ops["sy"]:
                f(e)
    kk.stack.close()
    return kk


def kernel(L0_attn_X, L0_attn_Wq, L0_attn_Wk, L0_attn_Wv, L0_attn_Wg,
           L0_attn_Wo, L0_attn_W1, L0_attn_W2, L0_attn_ln_scale,
           L0_attn_ln_bias, L0_attn_gn_scale, L0_attn_gn_bias,
           L0_ffn_combine_weights, L0_ffn_top_idx, L0_ffn_W1, L0_ffn_b1,
           L0_ffn_W2, L0_ffn_b2, L1_attn_Wx, L1_attn_Wz, L1_attn_conv_w,
           L1_attn_conv_b, L1_attn_Wb, L1_attn_Wc, L1_attn_Wdt, L1_attn_b_dt,
           L1_attn_log_a, L1_attn_d_param, L1_attn_gn_w, L1_attn_gn_b,
           L1_attn_Wout, L1_ffn_w1, L1_ffn_b1, L1_ffn_w2, L1_ffn_b2):
    Bx, S, Dx = L0_attn_X.shape
    assert Bx == B and Dx == D and S % (NC * CHK) == 0
    moe = moe_host_prep(np.asarray(L0_ffn_top_idx), S)
    moe["cwN"] = L0_ffn_combine_weights.shape[0]
    cst = host_constants(S)
    kk = build(S, moe)
    Sc = S // NC

    def bcast(v, n):
        return np.ascontiguousarray(np.broadcast_to(
            np.asarray(v, np.float32).reshape(1, -1), (128, n)))

    common = {
        "wq": np.ascontiguousarray(L0_attn_Wq, np.float32),
        "wk": np.ascontiguousarray(L0_attn_Wk, np.float32),
        "wv": np.ascontiguousarray(L0_attn_Wv, np.float32),
        "wg": np.ascontiguousarray(L0_attn_Wg, np.float32),
        "wo": np.ascontiguousarray(L0_attn_Wo, np.float32),
        "w1a": np.ascontiguousarray(L0_attn_W1, np.float32),
        "w2a": np.ascontiguousarray(L0_attn_W2, np.float32),
        "lnsb": bcast(L0_attn_ln_scale, D),
        "lnbb": bcast(L0_attn_ln_bias, D),
        "gnsb": np.ascontiguousarray(np.broadcast_to(
            np.asarray(L0_attn_gn_scale, np.float32)[:, None, :], (NL0, 128, D))),
        "gnbb": np.ascontiguousarray(np.broadcast_to(
            np.asarray(L0_attn_gn_bias, np.float32)[:, None, :], (NL0, 128, D))),
        "mw1": np.ascontiguousarray(L0_ffn_W1, np.float32),
        "mw2": np.ascontiguousarray(L0_ffn_W2, np.float32),
        "mb1": np.ascontiguousarray(np.broadcast_to(
            np.asarray(L0_ffn_b1, np.float32)[:, None, :], (EMO, 128, FMO))),
        "mb2": np.ascontiguousarray(np.broadcast_to(
            np.asarray(L0_ffn_b2, np.float32)[:, None, :], (EMO, 128, D))),
        "cw": np.ascontiguousarray(L0_ffn_combine_weights, np.float32),
        "wxT": np.ascontiguousarray(np.asarray(L1_attn_Wx, np.float32).T),
        "wzT": np.ascontiguousarray(np.asarray(L1_attn_Wz, np.float32).T),
        "wbT": np.ascontiguousarray(np.asarray(L1_attn_Wb, np.float32).T),
        "wcT": np.ascontiguousarray(np.asarray(L1_attn_Wc, np.float32).T),
        "wdtT": np.ascontiguousarray(np.asarray(L1_attn_Wdt, np.float32).T),
        "woutT": np.ascontiguousarray(np.asarray(L1_attn_Wout, np.float32).T),
        "convw": np.ascontiguousarray(np.asarray(L1_attn_conv_w, np.float32)[:, 0, :]),
        "convb": np.ascontiguousarray(np.asarray(L1_attn_conv_b, np.float32).reshape(16, 128).T),
        "bdt": np.ascontiguousarray(np.asarray(L1_attn_b_dt, np.float32).reshape(-1, 1)),
        "loga": np.ascontiguousarray(np.asarray(L1_attn_log_a, np.float32).reshape(-1, 1)),
        "dpar": bcast(L1_attn_d_param, H),
        "gnw2": bcast(L1_attn_gn_w, INNER),
        "gnb2": bcast(L1_attn_gn_b, INNER),
        "dw1T": np.ascontiguousarray(np.asarray(L1_ffn_w1, np.float32).T),
        "dw2T": np.ascontiguousarray(np.asarray(L1_ffn_w2, np.float32).T),
        "db1": bcast(L1_ffn_b1, FFD),
        "db2": bcast(L1_ffn_b2, D),
        "ident": cst["ident"],
        "tri": cst["tri"], "iota": cst["iota"],
        "Dq": cst["Dq"], "Dk": cst["Dk"], "gamcol": cst["gamcol"],
        "PP": cst["PP"], "QQ": cst["QQ"],
        "ones_1_128": cst["ones_1_128"], "ones8": cst["ones8"],
    }
    import ml_dtypes
    common["identbf"] = cst["identbf"].astype(ml_dtypes.bfloat16)
    X = np.ascontiguousarray(L0_attn_X, np.float32)
    in_maps = []
    mx = max(moe["chunks_t"] + [1])
    nwt = max(1, len(moe["work_tiles"]))
    for c in range(NC):
        pc = per_core_constants(cst, S, c)
        x0c = np.ascontiguousarray(
            X[:, c * Sc:(c + 1) * Sc, :].reshape(B * Sc, D))
        im = dict(common)
        im.update({
            "x0": x0c, "cosc": pc["cosc"], "sinc": pc["sinc"],
            "retw": pc["retw"], "m1col": pc["m1col"], "m2col": pc["m2col"],
            "ssd_i1": pc["ssd_i1"], "ssd_i2": pc["ssd_i2"],
            "ll_i1": pc["ll_i1"], "halo_i": pc["halo_i"],
            "gidx": moe["gidx"][c].reshape(-1, 1),
            "widx": moe["widx"][c].reshape(-1, 1),
            "sidx": moe["sidx"][c].reshape(-1, 1)[:nwt * mx * 128],
            "soff": moe["soff"][c].reshape(-1, 1)[:nwt * mx * 128],
        })
        in_maps.append(im)
    res = run_bass_kernel_spmd(kk.nc, in_maps, core_ids=list(range(NC)))
    out = np.empty((B, S, D), np.float32)
    for c in range(NC):
        out[:, c * Sc:(c + 1) * Sc, :] = res.results[c]["y"].reshape(B, Sc, D)
    return out
